# version 23
# baseline (speedup 1.0000x reference)
from contextlib import ExitStack

import numpy as np

import concourse.bass as bass
import concourse.mybir as mybir
from concourse.bass_utils import run_bass_kernel_spmd

F32 = mybir.dt.float32
BF16 = mybir.dt.bfloat16
F16 = mybir.dt.float16
I16 = mybir.dt.int16
ALU = mybir.AluOpType
AFT = mybir.ActivationFunctionType
AXL = mybir.AxisListType

MAGIC = float(np.float32(1.5 * 2 ** 23))
D = 1024
DT = 8
H = 4
DV = 256
CH = 64
MIT = 22
FIT = 32
DINT = 16
NST = 16
GLN = 16.0

ENGS = ("sync", "tensor", "vector", "scalar", "gpsimd")
ECH = {"sync": "cy", "tensor": "mm", "vector": "vx", "scalar": "sx", "gpsimd": "gx"}


class Prog:

    def __init__(self, nc, blk, stack):
        self.nc, self.blk, self.stack = nc, blk, stack
        self.q = {e: [] for e in ENGS}
        self.chan = {}
        self.last_wait = {e: {} for e in ENGS}
        self.dirty = {e: set() for e in ENGS}
        self.dirtyr = {e: set() for e in ENGS}
        self.wr = {}
        self.rd = {}
        self.untok = {e: 0 for e in ENGS}

    def mkchan(self, name):
        h = self.stack.enter_context(self.nc.semaphore(name))
        self.chan[name] = [h, 0]

    def cur(self, ch):
        return self.chan[ch][1]

    def wait_tok(self, eng, tok):
        ch, val = tok
        if val <= self.last_wait[eng].get(ch, 0):
            return
        h = self.chan[ch][0]
        self.q[eng].append(("w", h, val))
        self.last_wait[eng][ch] = val

    def e(self, eng, meth, *args, reads=(), writes=(), tok=False, dma_ch=None,
          **kwargs):
        reads, writes = tuple(reads), tuple(writes)
        rec_eng = f"@{dma_ch}" if dma_ch else eng
        for nm in reads:
            w = self.wr.get(nm)
            if w is not None and w[0] != rec_eng:
                if w[1] is None:
                    raise RuntimeError(f"RAW {nm}: writer on {w[0]} lacks token")
                self.wait_tok(eng, w[1])
        for nm in writes:
            w = self.wr.get(nm)
            if w is not None and w[0] != rec_eng:
                if w[1] is None:
                    raise RuntimeError(f"WAW {nm}: writer on {w[0]} lacks token")
                self.wait_tok(eng, w[1])
            last_per_eng = {}
            for (re, rt) in self.rd.get(nm, ()):
                if re != rec_eng:
                    last_per_eng[re] = rt
            for re, rt in last_per_eng.items():
                if rt is None:
                    raise RuntimeError(f"WAR {nm}: last reader on {re} lacks token")
                self.wait_tok(eng, rt)
        if not dma_ch:
            if (((set(reads) | set(writes)) & self.dirty[eng])
                    or (set(writes) & self.dirtyr[eng])):
                self.q[eng].append(("d",))
                self.dirty[eng] = set()
                self.dirtyr[eng] = set()
            self.dirty[eng] |= set(writes)
            self.dirtyr[eng] |= set(reads)
        token = None
        if tok or dma_ch:
            ch = dma_ch if dma_ch else ECH[eng]
            cc = self.chan[ch]
            cc[1] += 16 if dma_ch else 1
            token = (ch, cc[1])
            self.q[eng].append(("i", meth, args, kwargs, cc[0], 16 if dma_ch else 1))
            self.untok[eng] = 0
        else:
            self.q[eng].append(("o", meth, args, kwargs))
            if meth != "nop":
                self.untok[eng] += 1
        for nm in writes:
            self.wr[nm] = (rec_eng, token)
            self.rd[nm] = []
        for nm in reads:
            self.rd.setdefault(nm, []).append((rec_eng, token))
        return token

    def dma(self, eng, out_ap, in_ap, reads=(), writes=(), ch=None):
        ch = ch or ("dG" if eng == "gpsimd" else "dW")
        self.wait_tok(eng, (ch, self.chan[ch][1]))
        return self.e(eng, "dma_start", out_ap, in_ap,
                      reads=reads, writes=writes, dma_ch=ch)

    def barrier(self):
        for en in ENGS:
            assert self.untok[en] == 0, (
                f"barrier: engine {en} has {self.untok[en]} untokened trailing "
                f"ops; give the last one tok=True")
        marks = [("mm", self.cur("mm")), ("vx", self.cur("vx")),
                 ("sx", self.cur("sx")), ("gx", self.cur("gx")),
                 ("cy", self.cur("cy")),
                 ("dW", self.cur("dW")), ("dWb", self.cur("dWb")),
                 ("dG", self.cur("dG")), ("dGb", self.cur("dGb")),
                 ("dR", self.cur("dR")), ("dO", self.cur("dO"))]
        for en in ENGS:
            for (ch, v) in marks:
                if v:
                    self.wait_tok(en, (ch, v))
        self.wr = {}
        self.rd = {}

    def flush(self):
        for en in ENGS:
            ops = self.q[en]
            self.q[en] = []
            if not ops:
                continue

            def body(eng, ops=ops):
                for it in ops:
                    try:
                        if it[0] == "w":
                            eng.wait_ge(it[1], it[2])
                        elif it[0] == "d":
                            eng.drain()
                        elif it[0] == "o":
                            getattr(eng, it[1])(*it[2], **it[3])
                        else:
                            getattr(eng, it[1])(*it[2], **it[3]).then_inc(it[4], it[5])
                    except Exception as ex:
                        raise RuntimeError(
                            f"emit failed: {it[1] if it[0] in 'oi' else it[0]} "
                            f"args={[getattr(a, 'shape', a) for a in (it[2] if it[0] in 'oi' else ())]}"
                        ) from ex
            getattr(self.blk, en)(body)



def _panels(w, mt=128):
    K, M = w.shape
    return np.ascontiguousarray(
        w.reshape(K // 128, 128, M // mt, mt).transpose(2, 1, 0, 3))


def _kslabs(w):
    K, N = w.shape
    return np.ascontiguousarray(w.reshape(K // 128, 128, N))


def _kslabs_pm(w, halves=2):
    K, N = w.shape
    return np.ascontiguousarray(
        w.reshape(K // 128, 128, halves, N // halves).transpose(2, 1, 0, 3))


def _pcol(v):
    return np.ascontiguousarray(np.asarray(v).reshape(-1, 128).T)


def host_prepare(inp):
    f = lambda a: np.ascontiguousarray(np.asarray(a), dtype=np.float32)
    d = {}
    d["Wq_t"] = _panels(f(inp["L0_attn_a_Wq"]))
    d["Wk_t"] = _panels(f(inp["L0_attn_a_Wk"]))
    d["Wv_t"] = _kslabs_pm(f(inp["L0_attn_a_Wv"]))
    d["Wg_t"] = _kslabs_pm(f(inp["L0_attn_a_Wg"]))
    d["Wgk1_t"] = np.ascontiguousarray(
        f(inp["L0_attn_a_Wgk1"]).reshape(8, 128, 16).transpose(1, 0, 2))
    d["Wgk2_t"] = np.ascontiguousarray(
        f(inp["L0_attn_a_Wgk2"]).reshape(16, 4, 128).transpose(1, 0, 2))
    d["bgk2_c"] = _pcol(f(inp["L0_attn_a_bgk2"]))
    d["gw_c"] = _pcol(f(inp["L0_attn_a_gnorm_w"]))
    d["Wo_t"] = _kslabs_pm(f(inp["L0_attn_a_Wo"]))
    d["anw_c"] = _pcol(f(inp["L0_attn_a_attn_norm_w"]))
    d["mnw_c"] = _pcol(f(inp["L0_attn_a_mlp_norm_w"]))
    d["Wmg_t"] = _panels(f(inp["L0_attn_a_W_gate"]))
    d["Wmu_t"] = _panels(f(inp["L0_attn_a_W_up"]))
    d["Wmd_t"] = _panels(f(inp["L0_attn_a_W_down"]))
    d["fg_t"] = _panels(f(inp["L0_ffn_w_gate"]).T)
    d["fu_t"] = _panels(f(inp["L0_ffn_w_up"]).T)
    d["fd_t"] = _kslabs(f(inp["L0_ffn_w_down"]).T)
    d["Win_t"] = _panels(f(inp["L0_attn_b_W_in"]).T)
    d["cw_c"] = np.ascontiguousarray(
        f(inp["L0_attn_b_conv_w"]).reshape(16, 128, 4).transpose(1, 0, 2))
    d["cb_c"] = _pcol(f(inp["L0_attn_b_conv_b"]))
    wx = f(inp["L0_attn_b_W_x"])
    wxp_ = np.zeros((128, 2048), np.float32)
    wxp_[0:64] = wx[0:64]
    wxp_[64:80] = wx[64:80]
    wxp_[96:112] = wx[80:96]
    d["Wx_t"] = np.ascontiguousarray(
        wxp_.T.reshape(16, 128, 128).transpose(1, 0, 2))
    d["Wdt_t"] = np.ascontiguousarray(
        f(inp["L0_attn_b_W_dt"]).T.reshape(64, 16, 128))
    d["bdt_c"] = _pcol(f(inp["L0_attn_b_b_dt"]))
    d["alog_t"] = np.ascontiguousarray(
        f(inp["L0_attn_b_A_log"]).reshape(16, 128, 16).transpose(1, 0, 2))
    d["dp_c"] = _pcol(f(inp["L0_attn_b_D_param"]))
    d["Wout_t"] = _panels(f(inp["L0_attn_b_W_out"]).T)
    d["ident"] = np.eye(128, dtype=np.float32)
    d["onecol"] = np.ones((128, 1), dtype=np.float32)
    gm = np.ones((1, 512), dtype=np.float32)
    gm[0, ::CH] = 0.0
    d["gmask"] = gm
    si = np.arange(CH)
    d["smask"] = (si[:, None] <= si[None, :]).astype(np.float32)
    return d


DEBUG_OUT = False


class Ctx:
    pass



def mm_group(c, bank, mms, tile_position=None):
    P = c.P
    n = len(mms)
    union = []
    for (_, _, _, reads) in mms:
        for r in reads:
            if r not in union:
                union.append(r)
    for j, (lh, rh, out, reads) in enumerate(mms):
        kw = {}
        if tile_position is not None:
            kw["tile_position"] = tile_position
        rd = tuple(union) if j == n - 1 else tuple(reads)
        P.e("tensor", "matmul", out, lh, rh, start=(j == 0), stop=(j == n - 1),
            reads=rd, writes=(f"pb{bank}",), tok=(j == n - 1), **kw)


def ln_stats(c, src_ap, src_nm, want_mean=True):
    P = c.P
    TS = c.TS
    NJ = TS // 128
    NCHK = 2 if NJ >= 2 else 1
    HH = TS // NCHK
    nmm = NCHK * (NJ // NCHK) * (2 if want_mean else 1) * DT
    imm = 0
    for hh in range(NCHK):
        sl = src_ap[:, :, HH * hh:HH * hh + HH]
        if want_mean:
            P.e("scalar", "activation", c.stat_xb[:], sl, AFT.Copy,
                reads=(src_nm,), writes=("stat_xb",), tok=True)
        P.e("scalar", "activation", c.stat_sq[:], sl, AFT.Square,
            reads=(src_nm,), writes=("stat_sq",), tok=True)
        for jj in range(NJ // NCHK):
            j = hh * (NJ // NCHK) + jj
            srcs = ([(c.stat_xb, "stat_xb", j), (c.stat_sq, "stat_sq", NJ + j)]
                    if want_mean else [(c.stat_sq, "stat_sq", NJ + j)])
            for (buf, bnm, col) in srcs:
                for k in range(DT):
                    imm += 1
                    P.e("tensor", "matmul", c.pball[6][0:128, col:col + 1],
                        buf[:, k, 128 * jj:128 * (jj + 1)], c.onecolS[:],
                        start=(imm == 1), stop=(imm == nmm),
                        reads=(bnm, "onecolS"), writes=("pb6",),
                        tok=(k == DT - 1))


def ln_scalars(c, stm, stm_nm, eps=1e-5, want_mean=True, double_rms=False):
    P = c.P
    NJ = c.TS // 128
    w0 = stm[:, 0:NJ]
    w1 = stm[:, NJ:2 * NJ]
    w2 = stm[:, 2 * NJ:3 * NJ]
    w3 = stm[:, 3 * NJ:4 * NJ]
    if want_mean:
        P.e("scalar", "activation", w3, c.pball[6][0:128, 0:NJ], AFT.Copy,
            scale=1.0 / D, reads=("pb6",), writes=(stm_nm,), tok=True)
        P.e("scalar", "activation", w0, c.pball[6][0:128, NJ:2 * NJ], AFT.Copy,
            scale=1.0 / D, reads=("pb6",), writes=(stm_nm,), tok=True)
        P.e("vector", "tensor_tensor", w2, w3, w3, ALU.mult,
            reads=(stm_nm,), writes=(stm_nm,))
        P.e("vector", "tensor_tensor", w0, w0, w2, ALU.subtract,
            reads=(stm_nm,), writes=(stm_nm,))
    else:
        P.e("scalar", "activation", w0, c.pball[6][0:128, NJ:2 * NJ], AFT.Copy,
            scale=1.0 / D, reads=("pb6",), writes=(stm_nm,), tok=True)
    if double_rms:
        P.e("vector", "tensor_scalar", w2, w0, 1.0 + 1e-6, 1e-11,
            ALU.mult, ALU.add, reads=(stm_nm,), writes=(stm_nm,))
    P.e("vector", "tensor_scalar", w0, w0, 1.0, eps, ALU.mult, ALU.add,
        reads=(stm_nm,), writes=(stm_nm,), tok=True)
    P.e("scalar", "activation", w0, w0, AFT.Ln,
        reads=(stm_nm,), writes=(stm_nm,), tok=True)
    P.e("scalar", "activation", w0, w0, AFT.Exp, scale=-0.5,
        reads=(stm_nm,), writes=(stm_nm,), tok=True)
    if double_rms:
        P.e("scalar", "activation", w2, w2, AFT.Ln,
            reads=(stm_nm,), writes=(stm_nm,), tok=True)
        P.e("scalar", "activation", w2, w2, AFT.Exp, scale=-0.5,
            reads=(stm_nm,), writes=(stm_nm,), tok=True)
    if want_mean:
        P.e("vector", "tensor_tensor", w1, w3, w0, ALU.mult,
            reads=(stm_nm,), writes=(stm_nm,), tok=True)
    if double_rms:
        P.e("vector", "reciprocal", w3, w0, reads=(stm_nm,), writes=(stm_nm,))
        P.e("vector", "tensor_tensor", w2, w2, w3, ALU.mult,
            reads=(stm_nm,), writes=(stm_nm,), tok=True)


def replicate_row(c, dst_ap, dst_nm, row_ap, row_nm, cast=False):
    P = c.P
    shp = row_ap.shape
    n = int(np.prod(shp))
    slot = c.rep_slot
    c.rep_slot = (c.rep_slot + 1) % c.rep_nslot
    hbm = c.rows_hbm if row_ap.dtype == F32 else c.rowsb_hbm
    scr = hbm[slot:slot + 1, 0:n]
    snm = f"rowh{slot}_{row_ap.dtype}"
    if len(shp) == 2 and shp[0] > 1:
        scr_w = hbm[slot, 0:n].rearrange("(a b) -> a b", b=shp[1])
    else:
        scr_w = hbm[slot, 0:n].rearrange("t -> () t")
    P.dma("scalar", scr_w, row_ap, reads=(row_nm,), writes=(snm,), ch="dR")
    eng = "gpsimd" if cast else "scalar"
    ch = "dG" if cast else "dR"
    return P.dma(eng, dst_ap, scr.partition_broadcast(128),
                 reads=(snm,), writes=(dst_nm,), ch=ch)


def rep_tmcol(c, dst_ap, dst_nm, col_ap, col_nm, nj, cast=False):
    P = c.P
    P.e("tensor", "transpose", c.pball[6][0:nj, 0:128], col_ap, c.identS[:],
        reads=(col_nm, "identS"), writes=("pb6",), tok=True)
    P.e("scalar", "activation", c.rtmpS[0:nj, :], c.pball[6][0:nj, 0:128],
        AFT.Copy, reads=("pb6",), writes=("rtmpS",), tok=True)
    return replicate_row(c, dst_ap, dst_nm, c.rtmpS[0:nj, :], "rtmpS", cast=cast)


def load_panel(c, buf, buf_nm, src_ap, kt=8, par=0):
    return c.P.dma("gpsimd", buf[:, 0:kt, :], src_ap,
                   writes=(buf_nm,), ch=("dG", "dGb")[par])



def build_program(T=2048):
    c = Ctx()
    c.T = T
    c.TS = min(512, T)
    c.NS = T // c.TS
    c.NCH = c.TS // CH
    nc = bass.Bass("TRN2", target_bir_lowering=False, debug=False)
    c.nc = nc

    di = lambda n, shp: nc.dram_tensor(n, list(shp), F32, kind="ExternalInput").ap()
    c.x_in = di("x_in", [DT, 128, T])
    for n, shp in [
        ("Wq_t", [4, 128, 8, 128]), ("Wk_t", [4, 128, 8, 128]),
        ("Wv_t", [2, 128, 8, 512]), ("Wg_t", [2, 128, 8, 512]),
        ("Wgk1_t", [128, 8, 16]), ("Wgk2_t", [4, 16, 128]),
        ("bgk2_c", [128, 4]), ("gw_c", [128, 2]), ("Wo_t", [2, 128, 8, 512]),
        ("anw_c", [128, 8]), ("mnw_c", [128, 8]),
        ("Wmg_t", [MIT, 128, 8, 128]), ("Wmu_t", [MIT, 128, 8, 128]),
        ("Wmd_t", [DT, 128, MIT, 128]),
        ("fg_t", [FIT, 128, 8, 128]), ("fu_t", [FIT, 128, 8, 128]),
        ("fd_t", [FIT, 128, 1024]),
        ("Win_t", [FIT, 128, 8, 128]),
        ("cw_c", [128, 16, 4]), ("cb_c", [128, 16]),
        ("Wx_t", [128, 16, 128]), ("Wdt_t", [64, 16, 128]),
        ("bdt_c", [128, 16]), ("alog_t", [128, 16, 16]), ("dp_c", [128, 16]),
        ("Wout_t", [DT, 128, 16, 128]),
        ("ident", [128, 128]), ("onecol", [128, 1]),
        ("gmask", [1, 512]), ("smask", [CH, CH]),
    ]:
        setattr(c, n, di(n, shp))
    c.y_out = nc.dram_tensor("y_out", [DT, 128, T], F32, kind="ExternalOutput").ap()

    ikind = "ExternalOutput" if DEBUG_OUT else "Internal"
    dsc = lambda n, shp, dt=F32, kind="Internal": nc.dram_tensor(
        n, list(shp), dt, kind=kind).ap()
    c.x2_hbm = dsc("x2_hbm", [DT, 128, T], kind=ikind)
    c.x3_hbm = dsc("x3_hbm", [DT, 128, T], BF16, kind=ikind)
    c.wqg_hbm = dsc("wqg_hbm", [FIT, 128, 8, 128], F16, kind=ikind)
    c.wqu_hbm = dsc("wqu_hbm", [FIT, 128, 8, 128], F16)
    c.wqd_hbm = dsc("wqd_hbm", [FIT, 128, 1024], F16, kind=ikind)
    c.rep_nslot = 8
    c.rep_slot = 0
    c.rows_hbm = dsc("rows_hbm", [c.rep_nslot, NST * c.TS], F32)
    c.z_hbm = dsc("z_hbm", [2, DINT, 128, c.TS], BF16)
    c.rowsb_hbm = dsc("rowsb_hbm", [c.rep_nslot, NST * c.TS], BF16)

    stack = ExitStack()
    c.stack = stack
    c.sb = lambda n, shp, dt: stack.enter_context(nc.sbuf_tensor(n, list(shp), dt))
    c.pball = [stack.enter_context(nc.psum_tensor(f"pb{i}", [128, 512], F32))
               for i in range(7)]
    c.pbT = stack.enter_context(nc.psum_tensor("pbT", [128, 1024], BF16))

    x0_scope = ExitStack()
    c.x0 = x0_scope.enter_context(
        nc.sbuf_tensor("x0", [128, DT, T], F32, side="right"))
    c.identS = c.sb("identS", [128, 128], F32)
    c.identB = c.sb("identB", [128, 128], BF16)
    c.onecolS = c.sb("onecolS", [128, 1], BF16)
    c.onecolF = c.sb("onecolF", [128, 1], F32)
    c.gmaskS = c.sb("gmaskS", [128, c.TS], F32)
    c.smaskS = c.sb("smaskS", [CH, CH], F32)
    c.anwS = c.sb("anwS", [128, 8], F32)
    c.mnwS = c.sb("mnwS", [128, 8], F32)
    c.gwS = c.sb("gwS", [128, 2], F32)
    c.nbgS = c.sb("nbgS", [128, 4], F32)
    c.bdtS = c.sb("bdtS", [128, 16], F32)
    c.dpS = c.sb("dpS", [128, 16], F32)
    c.cwS = c.sb("cwS", [128, 16, 4], F32)
    c.cbS = c.sb("cbS", [128, 16], F32)
    c.A_sb = c.sb("A_sb", [128, 16, 16], F32)
    c.betaS = c.sb("betaS", [128, 8], F32)
    nchk = 2 if c.TS // 128 >= 2 else 1
    c.stat_xb = c.sb("stat_xb", [128, DT, c.TS // nchk], BF16)
    c.stat_sq = c.sb("stat_sq", [128, DT, c.TS // nchk], BF16)
    c.rtmpS = c.sb("rtmpS", [16, 128], F32)

    with nc.Block() as blk:
        P = Prog(nc, blk, stack)
        c.P = P
        for ch in ["cy", "mm", "vx", "sx", "gx", "dW", "dWb", "dG", "dGb",
                   "dR", "dO", "dX"]:
            P.mkchan(ch)

        phase_load(c)
        P.barrier(); P.flush()
        phase_attn(c)
        P.barrier(); P.flush()
        ws = ExitStack()
        wq_units = build_wq_units(c, ws)
        phase_mlp(c, wq_units=wq_units)
        P.barrier(); P.flush()
        ws.close()
        phase_ffn(c, src_hbm=None, dst_hbm=c.x2_hbm, tag="f1")
        P.barrier(); P.flush()
        x0_scope.close()
        phase_mamba(c)
        P.barrier(); P.flush()
        phase_ffn(c, src_hbm=c.x2_hbm, dst_hbm=c.y_out, tag="f2",
                  add_hbm=c.x3_hbm)
        P.wait_tok("sync", ("dO", P.cur("dO")))
        P.flush()

    stack.close()
    return nc


def phase_load(c):
    P = c.P
    for i in range(DT):
        P.dma("sync", c.x0[:, i, :], c.x_in[i], writes=("x0",))
    P.dma("sync", c.identS[:], c.ident, writes=("identS",))
    P.dma("sync", c.smaskS[:], c.smask, writes=("smaskS",))
    P.dma("sync", c.gmaskS[:], c.gmask[0:1, 0:c.TS].partition_broadcast(128),
          writes=("gmaskS",))
    for (dst, nm, src) in [(c.anwS, "anwS", c.anw_c), (c.mnwS, "mnwS", c.mnw_c),
                           (c.gwS, "gwS", c.gw_c), (c.nbgS, "nbgS", c.bgk2_c),
                           (c.bdtS, "bdtS", c.bdt_c), (c.dpS, "dpS", c.dp_c),
                           (c.cwS, "cwS", c.cw_c), (c.cbS, "cbS", c.cb_c)]:
        P.dma("sync", dst[:], src, writes=(nm,))
    P.dma("gpsimd", c.onecolS[:], c.onecol, writes=("onecolS",))
    P.dma("gpsimd", c.identB[:], c.ident, writes=("identB",))
    P.dma("sync", c.onecolF[:], c.onecol, writes=("onecolF",))
    alog = c.sb("alogS", [128, 16, 16], F32)
    P.dma("sync", alog[:], c.alog_t, writes=("alogS",))
    P.e("scalar", "activation", alog[:].rearrange("p a b -> p (a b)"),
        alog[:].rearrange("p a b -> p (a b)"), AFT.Exp,
        reads=("alogS",), writes=("alogS",), tok=True)
    P.e("vector", "tensor_scalar", c.A_sb[:].rearrange("p a b -> p (a b)"),
        alog[:].rearrange("p a b -> p (a b)"), -1.0, None, ALU.mult,
        reads=("alogS",), writes=("A_sb",), tok=True)
    P.e("vector", "tensor_scalar", c.nbgS[:], c.nbgS[:], -1.0, None, ALU.mult,
        reads=("nbgS",), writes=("nbgS",), tok=True)



def phase_attn(c):
    P, nc, pb = c.P, c.nc, c.pball
    T, TS, NS, NCH = c.T, c.TS, c.NS, c.NCH
    scale_dk = float(128 ** -0.5)

    with ExitStack() as ph:
        sb = lambda n, shp, dt: ph.enter_context(nc.sbuf_tensor(n, list(shp), dt))
        stm = sb("stm", [128, 4 * (TS // 128)], F32)
        cmc = sb("cmc", [128, 3, TS], F32)
        xln = sb("xln", [128, DT, TS], BF16)
        h_bf = sb("h_bf", [128, DT, TS], BF16)
        wpan = [sb(f"wpan{i}", [128, 8, 128], BF16) for i in range(2)]
        wgk2 = sb("wgk2", [16, 4, 128], BF16)
        wvg = sb("wvg", [128, 8, 512], BF16)
        qf = sb("qf", [128, 4, TS], BF16)
        kf = sb("kf", [128, 4, TS], BF16)
        spb = sb("spb", [128, 4, TS], BF16)
        t1b = sb("t1b", [16, TS], BF16)
        v_tm = sb("v_tm", [64, NCH, 1024], BF16)
        g_tm = sb("g_tm", [64, NCH, 1024], BF16)
        cum = sb("cum", [128, TS], F32)
        ebp = sb("ebp", [128, TS], F32)
        ebn = sb("ebn", [128, TS], F32)
        qe = [sb(f"qe{i}", [128, TS], BF16) for i in range(2)]
        ke_bf = [sb(f"ke_bf{i}", [128, TS], BF16) for i in range(2)]
        ebl = [sb(f"ebl{i}", [128, NCH], F32) for i in range(2)]
        am = [sb(f"am{i}", [64, CH], BF16) for i in range(2)]
        kdtm = [sb(f"kdtm{i}", [64, 128], BF16) for i in range(2)]
        sst = [sb(f"sst{h}", [128, DV], F32) for h in range(H)]
        sstb = [sb(f"sstb{h}", [128, DV], BF16) for h in range(H)]
        sst_t = sb("sst_t", [128, DV], F32)
        o_tm = sb("o_tm", [64, NCH, 1024], BF16)
        osq = sb("osq", [64, NCH // 4 if NCH >= 4 else NCH, 1024], BF16)
        ssq = sb("ssq", [64, NCH, 4], F32)
        ogT = sb("ogT", [128, 8, TS], BF16)
        x2t = sb("x2t", [128, TS], F32)

        for s in range(NS):
            ts = s * TS
            x0sl = c.x0[:, :, ts:ts + TS]

            NJ = TS // 128
            ln_stats(c, x0sl, "x0")
            ln_scalars(c, stm[:], "stm", eps=1e-5, want_mean=True,
                       double_rms=True)
            rep_tmcol(c, cmc[:], "cmc", stm[:, 0:3 * NJ], "stm", 3 * NJ)
            for i in range(DT):
                P.e("vector", "scalar_tensor_tensor", x2t[:],
                    c.x0[:, i, ts:ts + TS], 1.0, cmc[:, 0, :], ALU.mult, ALU.mult,
                    reads=("x0", "cmc"), writes=("x2t",))
                P.e("vector", "tensor_tensor", xln[:, i, :], x2t[:],
                    cmc[:, 1, :], ALU.subtract,
                    reads=("x2t", "cmc"), writes=("xln",),
                    tok=(i == DT - 1))
                P.e("vector", "scalar_tensor_tensor", h_bf[:, i, :], xln[:, i, :],
                    c.anwS[:, i:i + 1], cmc[:, 2, :], ALU.mult, ALU.mult,
                    reads=("xln", "anwS", "cmc"), writes=("h_bf",),
                    tok=(i == DT - 1))

            P.dma("gpsimd", wpan[0][:, 0:8, 0:16], c.Wgk1_t,
                  writes=("wpan0",))
            mm_group(c, 0, [(wpan[0][:, k, 0:16], h_bf[:, k, :],
                             pb[0][0:16, 0:TS], ("wpan0", "h_bf"))
                            for k in range(8)])
            P.e("scalar", "activation", t1b[:], pb[0][0:16, 0:TS], AFT.Copy,
                reads=("pb0",), writes=("t1b",), tok=True)
            P.dma("gpsimd", wgk2[:], c.Wgk2_t.rearrange("a p m -> p a m"),
                  writes=("wgk2",))
            for m in range(4):
                mm_group(c, 1, [(wgk2[0:16, m, :], t1b[:], pb[1][:, 0:TS],
                                 ("wgk2", "t1b"))])
                P.e("scalar", "activation", spb[:, m, :], pb[1][:, 0:TS], AFT.Exp,
                    scale=-1.0, bias=c.nbgS[:, m:m + 1],
                    reads=("pb1", "nbgS"), writes=("spb",), tok=True)
            spf = spb[:].rearrange("p a t -> p (a t)")
            P.e("vector", "tensor_scalar", spf, spf, 1.0, None, ALU.add,
                reads=("spb",), writes=("spb",), tok=True)
            P.e("scalar", "activation", spf, spf, AFT.Ln,
                reads=("spb",), writes=("spb",), tok=True)

            for (wt, dst, nm) in ((c.Wq_t, qf, "qf"), (c.Wk_t, kf, "kf")):
                for m in range(4):
                    buf, bn = wpan[m % 2], f"wpan{m % 2}"
                    load_panel(c, buf, bn, wt[m], par=m % 2)
                    bi = 2 + (m % 2)
                    mm_group(c, bi, [(buf[:, k, :], h_bf[:, k, :],
                                      pb[bi][:, 0:TS], (bn, "h_bf"))
                                     for k in range(8)])
                    P.e("scalar", "activation", dst[:, m, :], pb[bi][:, 0:TS],
                        AFT.Copy, reads=(f"pb{bi}",), writes=(nm,), tok=True)

            for (wt, dst, nm) in ((c.Wv_t, v_tm, "v_tm"), (c.Wg_t, g_tm, "g_tm")):
                for half in range(2):
                    P.dma("gpsimd", wvg[:], wt[half], writes=("wvg",))
                    for j in range(TS // 128):
                        bi = 4 + half
                        mm_group(c, bi, [(h_bf[:, k, 128 * j:128 * (j + 1)],
                                          wvg[:, k, :],
                                          pb[bi][:, 0:512], ("h_bf", "wvg"))
                                         for k in range(8)])
                        for q in range(2):
                            cidx = 2 * j + q
                            P.e("scalar", "activation",
                                dst[0:64, cidx, 512 * half:512 * (half + 1)],
                                pb[bi][64 * q:64 * q + 64, 0:512], AFT.Copy,
                                reads=(f"pb{bi}",), writes=(nm,), tok=True)

            if s == 0:
                for h in range(H):
                    P.e("vector", "memset", sst[h][:], 0.0,
                        writes=(f"sst{h}",), tok=True)
                    P.e("vector", "memset", sstb[h][:], 0.0,
                        writes=(f"sstb{h}",), tok=True)
            cumv = cum[:].rearrange("p (a q) -> p a q", q=CH)
            for h in range(H):
                hb = h % 2
                qeh, keh = qe[hb], ke_bf[hb]
                eblh, amh, kdh = ebl[hb], am[hb], kdtm[hb]
                qn, kn = f"qe{hb}", f"ke_bf{hb}"
                en, an, dn = f"ebl{hb}", f"am{hb}", f"kdtm{hb}"
                P.e("vector", "tensor_tensor_scan", cum[:], c.gmaskS[:],
                    spb[:, h, :], 0.0, ALU.mult, ALU.add,
                    reads=("gmaskS", "spb"), writes=("cum",), tok=True)
                P.e("scalar", "activation", ebp[:], cum[:], AFT.Exp,
                    scale=-1.0 / GLN, reads=("cum",), writes=("ebp",), tok=True)
                P.e("scalar", "activation", ebn[:], cum[:], AFT.Exp,
                    scale=1.0 / GLN, reads=("cum",), writes=("ebn",), tok=True)
                P.e("scalar", "activation", eblh[:, 0:NCH], cumv[:, :, CH - 1],
                    AFT.Exp, scale=-1.0 / GLN,
                    reads=("cum",), writes=(en,), tok=True)
                P.e("vector", "scalar_tensor_tensor", qeh[:], qf[:, h, :],
                    scale_dk, ebp[:], ALU.mult, ALU.mult,
                    reads=("qf", "ebp"), writes=(qn,), tok=True)
                P.e("vector", "tensor_tensor", keh[:], kf[:, h, :], ebn[:],
                    ALU.mult, reads=("kf", "ebn"), writes=(kn,), tok=True)
                for ci in range(NCH):
                    t0 = ci * CH
                    mm_group(c, 6, [(keh[:, t0:t0 + CH], qeh[:, t0:t0 + CH],
                                     pb[6][0:64, 0:CH], (kn, qn))])
                    P.e("tensor", "transpose", c.pbT[0:64, 0:128],
                        keh[:, t0:t0 + CH], c.identB[:],
                        reads=(kn, "identB"), writes=("pbT",), tok=True)
                    P.e("vector", "tensor_tensor", amh[:], pb[6][0:64, 0:CH],
                        c.smaskS[:], ALU.mult,
                        reads=("pb6", "smaskS"), writes=(an,), tok=True)
                    P.e("scalar", "activation", kdh[:], c.pbT[0:64, 0:128],
                        AFT.Copy, reads=("pbT",), writes=(dn,), tok=True)
                    vch = v_tm[0:64, ci, DV * h:DV * (h + 1)]
                    mm_group(c, 5, [
                        (amh[:, :], vch, pb[5][0:64, 0:DV], (an, "v_tm")),
                        (qeh[:, t0:t0 + CH], sstb[h][:], pb[5][0:64, 0:DV],
                         (qn, f"sstb{h}")),
                    ])
                    mm_group(c, 4, [(kdh[:], vch, pb[4][0:128, 0:DV],
                                     (dn, "v_tm"))])
                    P.e("scalar", "activation",
                        o_tm[0:64, ci, DV * h:DV * (h + 1)], pb[5][0:64, 0:DV],
                        AFT.Copy, reads=("pb5",), writes=("o_tm",), tok=True)
                    P.e("vector", "tensor_tensor", sst_t[:], sst[h][:],
                        pb[4][0:128, 0:DV], ALU.add,
                        reads=(f"sst{h}", "pb4"), writes=("sst_t",), tok=True)
                    P.e("vector", "tensor_scalar", sst[h][:], sst_t[:],
                        eblh[:, ci:ci + 1], None, ALU.mult,
                        reads=("sst_t", en), writes=(f"sst{h}",), tok=True)
                    P.e("scalar", "activation", sstb[h][:], sst[h][:], AFT.Copy,
                        reads=(f"sst{h}",), writes=(f"sstb{h}",), tok=True)

            NQ = 4 if NCH >= 4 else 1
            HB = NCH // NQ
            for half in range(NQ):
                ovh = o_tm[0:64, HB * half:HB * half + HB, :]
                P.e("scalar", "activation",
                    osq[0:64, 0:HB, :].rearrange("p c d -> p (c d)"),
                    ovh.rearrange("p c d -> p (c d)"), AFT.Square,
                    reads=("o_tm",), writes=("osq",), tok=True)
                P.e("vector", "tensor_reduce",
                    ssq[:, HB * half:HB * half + HB, :]
                    .rearrange("p c h -> p (c h)").unsqueeze(2),
                    osq[0:64, 0:HB, :].rearrange("p c (h d) -> p (c h) d", d=DV),
                    AXL.X, ALU.add, reads=("osq",), writes=("ssq",),
                    tok=True)
            ssqf = ssq[:].rearrange("p c h -> p (c h)")
            P.e("vector", "tensor_scalar", ssqf, ssqf, 1.0 / DV, 1e-6,
                ALU.mult, ALU.add, reads=("ssq",), writes=("ssq",), tok=True)
            P.e("scalar", "activation", ssqf, ssqf, AFT.Ln,
                reads=("ssq",), writes=("ssq",), tok=True)
            P.e("scalar", "activation", ssqf, ssqf, AFT.Exp, scale=-0.5,
                reads=("ssq",), writes=("ssq",), tok=True)
            gf = g_tm[0:64, :, :].rearrange("p c d -> p (c d)")
            P.e("scalar", "activation", gf, gf, AFT.Silu,
                reads=("g_tm",), writes=("g_tm",), tok=True)
            P.e("vector", "tensor_tensor",
                o_tm[0:64, :, :].rearrange("p c (h d) -> p c h d", d=DV),
                o_tm[0:64, :, :].rearrange("p c (h d) -> p c h d", d=DV),
                ssq[:].unsqueeze(3).broadcast_to([64, NCH, 4, DV]), ALU.mult,
                reads=("o_tm", "ssq"), writes=("o_tm",))
            ov = o_tm[0:64, :, :].rearrange("p c d -> p (c d)")
            P.e("vector", "tensor_tensor", ov, ov, gf, ALU.mult,
                reads=("o_tm", "g_tm"), writes=("o_tm",), tok=True)
            for j in range(8):
                for ci in range(NCH):
                    P.e("tensor", "transpose",
                        c.pbT[0:128, CH * ci:CH * ci + CH],
                        o_tm[0:64, ci, 128 * j:128 * (j + 1)],
                        c.identB[0:64, 0:64], reads=("o_tm", "identB"),
                        writes=("pbT",), tok=(ci == NCH - 1))
                P.e("scalar", "activation", ogT[:, j, :], c.pbT[0:128, 0:TS],
                    AFT.Copy, reads=("pbT",), writes=("ogT",), tok=True)

            for half in range(2):
                P.dma("gpsimd", wvg[:], c.Wo_t[half], writes=("wvg",))
                for k in range(8):
                    P.e("vector", "tensor_scalar", wvg[:, k, :], wvg[:, k, :],
                        c.gwS[:, k % 2:k % 2 + 1], None, ALU.mult,
                        reads=("wvg", "gwS"), writes=("wvg",), tok=(k == 7))
                for m in range(4 * half, 4 * half + 4):
                    bi = m % 2
                    mm_group(c, bi, [(wvg[:, k, 128 * m - 512 * half:
                                          128 * (m + 1) - 512 * half],
                                      ogT[:, k, :],
                                      pb[bi][:, 0:TS], ("wvg", "ogT"))
                                     for k in range(8)])
                    P.e("vector", "tensor_tensor", x2t[:], pb[bi][:, 0:TS],
                        xln[:, m, :], ALU.add,
                        reads=(f"pb{bi}", "xln"), writes=("x2t",), tok=True)
                    P.dma("sync", c.x2_hbm[m, :, ts:ts + TS], x2t[:],
                          reads=("x2t",), ch="dW")


def phase_mlp(c, wq_units=()):
    P, nc, pb = c.P, c.nc, c.pball
    T, TS, NS = c.T, c.TS, c.NS
    wqit = iter(wq_units)
    wq_next = lambda: next(wqit, lambda: None)()
    with ExitStack() as ph:
        sb = lambda n, shp, dt: ph.enter_context(nc.sbuf_tensor(n, list(shp), dt))
        x2s = sb("ml_x2s", [128, DT, TS], F32)
        stm = sb("ml_stm", [128, 4 * (TS // 128)], F32)
        c2b = sb("ml_c2b", [128, TS], F32)
        nmb = sb("ml_nmb", [128, DT, TS], BF16)
        gs16 = sb("ml_gs16", [128, TS], F16)
        m_in = sb("ml_min", [128, MIT, TS], BF16)
        mpan = [sb(f"ml_mpan{i}", [128, 8, 128], BF16) for i in range(2)]
        upan = [sb(f"ml_upan{i}", [128, 8, 128], BF16) for i in range(2)]
        dpan = [sb(f"ml_dpan{i}", [128, MIT, 128], BF16) for i in range(2)]

        for s in range(NS):
            ts = s * TS
            for i in range(DT):
                P.dma("sync", x2s[:, i, :], c.x2_hbm[i, :, ts:ts + TS],
                      writes=("ml_x2s",))
            NJ = TS // 128
            ln_stats(c, x2s[:], "ml_x2s", want_mean=False)
            ln_scalars(c, stm[:], "ml_stm", eps=1e-6, want_mean=False)
            rep_tmcol(c, c2b[:], "ml_c2b", stm[:, 0:NJ], "ml_stm", NJ)
            for i in range(DT):
                P.e("vector", "scalar_tensor_tensor", nmb[:, i, :], x2s[:, i, :],
                    c.mnwS[:, i:i + 1], c2b[:], ALU.mult, ALU.mult,
                    reads=("ml_x2s", "mnwS", "ml_c2b"), writes=("ml_nmb",),
                    tok=(i == DT - 1))
            if s == 0:
                load_panel(c, mpan[0], "ml_mpan0", c.Wmg_t[0], par=0)
                load_panel(c, upan[0], "ml_upan0", c.Wmu_t[0], par=1)
            for m in range(MIT):
                gbuf, gbn = mpan[m % 2], f"ml_mpan{m % 2}"
                ubuf, ubn = upan[m % 2], f"ml_upan{m % 2}"
                if m + 1 < MIT:
                    load_panel(c, mpan[(m + 1) % 2], f"ml_mpan{(m + 1) % 2}",
                               c.Wmg_t[m + 1], par=0)
                    load_panel(c, upan[(m + 1) % 2], f"ml_upan{(m + 1) % 2}",
                               c.Wmu_t[m + 1], par=1)
                elif s + 1 < NS:
                    load_panel(c, mpan[0], "ml_mpan0", c.Wmg_t[0], par=0)
                    load_panel(c, upan[0], "ml_upan0", c.Wmu_t[0], par=1)
                bi = m % 3
                mm_group(c, bi, [(gbuf[:, k, :], nmb[:, k, :], pb[bi][:, 0:TS],
                                  (gbn, "ml_nmb")) for k in range(8)])
                P.e("scalar", "activation", gs16[:], pb[bi][:, 0:TS], AFT.Silu,
                    reads=(f"pb{bi}",), writes=("ml_gs16",), tok=True)
                bi2 = 3 + m % 3
                mm_group(c, bi2, [(ubuf[:, k, :], nmb[:, k, :], pb[bi2][:, 0:TS],
                                   (ubn, "ml_nmb")) for k in range(8)])
                P.e("vector", "tensor_tensor", m_in[:, m, :], gs16[:],
                    pb[bi2][:, 0:TS], ALU.mult,
                    reads=("ml_gs16", f"pb{bi2}"), writes=("ml_min",), tok=True)
                wq_next()
            if s == 0:
                P.dma("gpsimd", dpan[0][:], c.Wmd_t[0], writes=("ml_dpan0",),
                      ch="dG")
            for m in range(DT):
                buf, bn = dpan[m % 2], f"ml_dpan{m % 2}"
                if m + 1 < DT:
                    P.dma("gpsimd", dpan[(m + 1) % 2][:], c.Wmd_t[m + 1],
                          writes=(f"ml_dpan{(m + 1) % 2}",),
                          ch=("dG", "dGb")[(m + 1) % 2])
                elif s + 1 < NS:
                    P.dma("gpsimd", dpan[0][:], c.Wmd_t[0], writes=("ml_dpan0",),
                          ch="dG")
                bi = 4 + m % 2
                mm_group(c, bi, [(buf[:, k, :], m_in[:, k, :], pb[bi][:, 0:TS],
                                  (bn, "ml_min")) for k in range(MIT)])
                P.e("vector", "tensor_tensor", x2s[:, m, :], pb[bi][:, 0:TS],
                    x2s[:, m, :], ALU.add,
                    reads=(f"pb{bi}", "ml_x2s"), writes=("ml_x2s",), tok=True)
                P.e("vector", "tensor_tensor", c.x0[:, m, ts:ts + TS],
                    c.x0[:, m, ts:ts + TS], x2s[:, m, :], ALU.add,
                    reads=("x0", "ml_x2s"), writes=("x0",), tok=True)
                wq_next()
        for u in wqit:
            u()


def build_wq_units(c, ph):
    P, nc, pb = c.P, c.nc, c.pball
    units = []
    if True:
        sb = lambda n, shp, dt: ph.enter_context(nc.sbuf_tensor(n, list(shp), dt))
        w1 = [sb(f"qw1{i}", [128, 16, 128], F32) for i in range(2)]
        w2 = [sb(f"qw2{i}", [128, 16, 128], F32) for i in range(2)]
        wq16 = [sb(f"wq16{i}", [128, 16, 128], F16) for i in range(2)]
        qdmy = [sb(f"qdmy{i}", [128, 16, 128], F16) for i in range(2)]
        acc = [sb(f"qacc{i}", [128, FIT // 2], F32) for i in range(2)]
        accr = sb("qaccr", [128, 1], F32)
        qrow = sb("qrow", [1, 4], F32)
        jobs = [("fg", c.fg_t, c.wqg_hbm, True, 0, 3),
                ("fu", c.fu_t, c.wqu_hbm, True, 1, 4),
                ("fd", c.fd_t, c.wqd_hbm, False, 2, 5)]

        def dload(buf, bn, src_t, ispan, j, ch):
            if ispan:
                P.dma("sync", buf[:].rearrange("p (a k) m -> p a k m", a=2),
                      src_t[2 * j:2 * j + 2].rearrange("a p k m -> p a k m"),
                      writes=(bn,), ch=ch)
            else:
                P.dma("sync", buf[:, 0:8, :].rearrange("p k m -> p (k m)"),
                      src_t[2 * j], writes=(bn,), ch=ch)
                P.dma("sync", buf[:, 8:16, :].rearrange("p k m -> p (k m)"),
                      src_t[2 * j + 1], writes=(bn,), ch=ch)

        def p1_panel(job, jp, j, first):
            def emit():
                (nm, src_t, dst, ispan, bcol, mcol) = job
                an = f"qacc{jp}"
                if first:
                    P.e("vector", "memset", acc[jp][:], 0.0, writes=(an,),
                        tok=True)
                buf, bn = w1[j % 2], f"qw1{j % 2}"
                dload(buf, bn, src_t, ispan, j, "dW" if j % 2 else "dWb")
                P.e("scalar", "activation",
                    qdmy[j % 2][:].rearrange("p k m -> p (k m)"),
                    buf[:].rearrange("p k m -> p (k m)"), AFT.Abs,
                    accum_out=acc[jp][:, j:j + 1],
                    reads=(bn,), writes=(an, f"qdmy{j % 2}"), tok=True)
            return emit

        def p1_tail(job, jp):
            def emit():
                (nm, src_t, dst, ispan, bcol, mcol) = job
                an = f"qacc{jp}"
                P.e("vector", "tensor_reduce", accr[:], acc[jp][:], AXL.X,
                    ALU.add, reads=(an,), writes=("qaccr",), tok=True)
                P.e("tensor", "matmul", pb[6][0:1, 0:1], c.onecolF[:], accr[:],
                    start=True, stop=True,
                    reads=("qaccr", "onecolF"), writes=("pb6",), tok=True)
                nelem = float(FIT * 8 * 128 * 128)
                P.e("vector", "tensor_scalar", qrow[0:1, 0:1], pb[6][0:1, 0:1],
                    1.0 / nelem, None, ALU.mult, reads=("pb6",),
                    writes=("qrow",), tok=True)
                P.e("vector", "tensor_scalar", qrow[0:1, 0:1], qrow[0:1, 0:1],
                    1e-5, None, ALU.max, reads=("qrow",), writes=("qrow",))
                P.e("vector", "reciprocal", qrow[0:1, 1:2], qrow[0:1, 0:1],
                    reads=("qrow",), writes=("qrow",), tok=True)
                replicate_row(c, c.betaS[:, job[4]:job[4] + 1], "betaS",
                              qrow[0:1, 1:2], "qrow")
                replicate_row(c, c.betaS[:, job[5]:job[5] + 1], "betaS",
                              qrow[0:1, 0:1], "qrow")
            return emit

        def p2_panel(job, j):
            def emit():
                (nm, src_t, dst, ispan, bcol, mcol) = job
                buf, bn = w2[j % 2], f"qw2{j % 2}"
                dload(buf, bn, src_t, ispan, j, "dR" if j % 2 else "dX")
                v = buf[:].rearrange("p k m -> p (k m)")
                q = wq16[j % 2][:].rearrange("p k m -> p (k m)")
                qn = f"wq16{j % 2}"
                P.e("vector", "tensor_scalar", v, v, c.betaS[:, bcol:bcol + 1],
                    MAGIC, ALU.mult, ALU.add,
                    reads=(bn, "betaS"), writes=(bn,))
                P.e("vector", "tensor_scalar", v, v, MAGIC, None, ALU.subtract,
                    reads=(bn, "betaS"), writes=(bn,), tok=True)
                P.e("vector", "tensor_scalar", q, v, 1.0, -1.0, ALU.min,
                    ALU.max, reads=(bn,), writes=(qn,), tok=True)
                if ispan:
                    P.dma("sync",
                          dst[2 * j:2 * j + 2].rearrange("a p k m -> p a k m"),
                          wq16[j % 2][:].rearrange("p (a k) m -> p a k m", a=2),
                          reads=(qn,), ch="dO")
                else:
                    P.dma("sync", dst[2 * j],
                          wq16[j % 2][:, 0:8, :].rearrange("p k m -> p (k m)"),
                          reads=(qn,), ch="dO")
                    P.dma("sync", dst[2 * j + 1],
                          wq16[j % 2][:, 8:16, :].rearrange("p k m -> p (k m)"),
                          reads=(qn,), ch="dO")
            return emit

        for ji, job in enumerate(jobs):
            jp = ji % 2
            for j in range(FIT // 2):
                units.append(p1_panel(job, jp, j, first=(j == 0)))
            units.append(p1_tail(job, jp))
            for j in range(FIT // 2):
                units.append(p2_panel(job, j))
    return units



def phase_ffn(c, src_hbm, dst_hbm, tag, add_hbm=None):
    P, nc, pb = c.P, c.nc, c.pball
    T, TS, NS = c.T, c.TS, c.NS
    NJ = TS // 128
    with ExitStack() as ph:
        sb = lambda n, shp, dt: ph.enter_context(
            nc.sbuf_tensor(f"{tag}{n}", list(shp), dt))
        nb = lambda n: f"{tag}{n}"
        xs = sb("xs", [128, DT, TS], F32)
        stm = sb("stm", [128, 4 * (TS // 128)], F32)
        cmc = sb("cmc", [128, 2, TS], F32)
        xl16 = sb("xl16", [128, DT, TS], F16)
        macc = sb("macc", [128, TS], F32)
        mxq = sb("mxq", [128, NJ], F32)
        rq = sb("rq", [128, NJ], F32)
        sq_ = sb("sq", [128, NJ], F32)
        rqb = sb("rqb", [128, TS], F16)
        gsb = sb("gsb", [128, TS], F32)
        xqi = sb("xqi", [128, TS], I16)
        xq16 = sb("xq16", [128, DT, TS], F16)
        wq = [sb(f"wq{i}", [128, 8, 128], F16) for i in range(4)]
        wu = [sb(f"wu{i}", [128, 8, 128], F16) for i in range(4)]
        wd = [sb(f"wd{i}", [128, 1024], F16) for i in range(4)]
        gs = sb("gs", [128, TS], F32)
        us = sb("us", [128, TS], F16)
        y16 = sb("y16", [128, FIT, TS], F16)
        ymacc = sb("ymacc", [128, TS], F16)
        ymn = sb("ymn", [128, TS], F16)
        ymx = sb("ymx", [128, NJ], F32)
        yfac = sb("yfac", [128, NJ], F32)
        ysc = sb("ysc", [128, NJ], F32)
        yrb = sb("yrb", [128, TS], F16)
        ysb = sb("ysb", [128, TS], F32)
        yqi = sb("yqi", [128, TS], I16)
        yq16 = y16
        if add_hbm is not None:
            dtmp = sb("dtmp", [128, 2, TS], BF16)

        for s in range(NS):
            ts = s * TS
            if src_hbm is None:
                xsl, xsl_nm = c.x0[:, :, ts:ts + TS], "x0"
            else:
                for i in range(DT):
                    P.dma("sync", xs[:, i, :], src_hbm[i, :, ts:ts + TS],
                          writes=(nb("xs"),))
                if add_hbm is not None:
                    for i in range(DT):
                        P.dma("sync", dtmp[:, i % 2, :],
                              add_hbm[i, :, ts:ts + TS],
                              writes=(nb("dtmp"),), ch="dWb")
                        P.e("vector", "tensor_tensor", xs[:, i, :],
                            dtmp[:, i % 2, :], xs[:, i, :], ALU.add,
                            reads=(nb("dtmp"), nb("xs")), writes=(nb("xs"),),
                            tok=True)
                xsl, xsl_nm = xs[:], nb("xs")

            ln_stats(c, xsl, xsl_nm)
            ln_scalars(c, stm[:], nb("stm"), eps=1e-5, want_mean=True)
            rep_tmcol(c, cmc[:], nb("cmc"), stm[:, 0:2 * NJ], nb("stm"), 2 * NJ)
            for i in range(DT):
                P.e("vector", "tensor_tensor", macc[:], xsl[:, i, :],
                    cmc[:, 0, :], ALU.mult,
                    reads=(xsl_nm, nb("cmc")), writes=(nb("macc"),),
                    tok=(i == DT - 1))
                P.e("vector", "tensor_tensor", xl16[:, i, :], macc[:],
                    cmc[:, 1, :], ALU.subtract,
                    reads=(nb("macc"), nb("cmc")),
                    writes=(nb("xl16"),), tok=(i == DT - 1))

            P.e("scalar", "activation", macc[:], xl16[:, 0, :], AFT.Abs,
                reads=(nb("xl16"),), writes=(nb("macc"),), tok=True)
            for i in range(1, DT):
                P.e("scalar", "activation", gs[:], xl16[:, i, :], AFT.Abs,
                    reads=(nb("xl16"),), writes=(nb("gs"),), tok=True)
                P.e("vector", "tensor_tensor", macc[:], macc[:], gs[:],
                    ALU.max, reads=(nb("macc"), nb("gs")),
                    writes=(nb("macc"),), tok=True)
            for j in range(NJ):
                P.e("tensor", "transpose", pb[6][0:128, 0:128],
                    macc[:, 128 * j:128 * (j + 1)], c.identS[:],
                    reads=(nb("macc"), "identS"), writes=("pb6",), tok=True)
                P.e("vector", "tensor_reduce", mxq[:, j:j + 1].unsqueeze(1),
                    pb[6][0:128, 0:128].unsqueeze(1), AXL.X, ALU.max,
                    reads=("pb6",), writes=(nb("mxq"),), tok=True)
            P.e("vector", "tensor_scalar", mxq[:], mxq[:], 1e-5, None, ALU.max,
                reads=(nb("mxq"),), writes=(nb("mxq"),))
            P.e("vector", "reciprocal", rq[:], mxq[:],
                reads=(nb("mxq"),), writes=(nb("rq"),))
            P.e("vector", "tensor_scalar", rq[:], rq[:], 127.0, None, ALU.mult,
                reads=(nb("rq"),), writes=(nb("rq"),), tok=True)
            P.e("vector", "tensor_scalar", sq_[:], mxq[:], 1.0 / 127.0, None,
                ALU.mult, reads=(nb("mxq"),), writes=(nb("sq"),), tok=True)
            rep_tmcol(c, rqb[:], nb("rqb"), rq[:], nb("rq"), NJ, cast=True)
            rep_tmcol(c, gsb[:], nb("gsb"), sq_[:], nb("sq"), NJ)
            P.e("vector", "tensor_scalar", gsb[:], gsb[:], c.betaS[:, 3:4], None,
                ALU.mult, reads=(nb("gsb"), "betaS"), writes=(nb("gsb"),),
                tok=True)
            for i in range(DT):
                P.e("vector", "tensor_tensor", macc[:], xl16[:, i, :], rqb[:],
                    ALU.mult, reads=(nb("xl16"), nb("rqb")), writes=(nb("macc"),))
                P.e("vector", "tensor_scalar", macc[:], macc[:], MAGIC, None,
                    ALU.add, reads=(nb("macc"),), writes=(nb("macc"),))
                P.e("vector", "tensor_scalar", xqi[:], macc[:], MAGIC, None,
                    ALU.subtract, reads=(nb("macc"),), writes=(nb("xqi"),))
                P.e("vector", "tensor_copy", xq16[:, i, :], xqi[:],
                    reads=(nb("xqi"), nb("rqb"), nb("macc")),
                    writes=(nb("xq16"),), tok=(i == DT - 1))

            for j in range(3):
                P.dma("sync", wq[j][:], c.wqg_hbm[j], writes=(nb(f"wq{j}"),),
                      ch=("dW", "dR")[j % 2])
                P.dma("sync", wu[j][:], c.wqu_hbm[j], writes=(nb(f"wu{j}"),),
                      ch=("dWb", "dX")[j % 2])
            for m in range(FIT):
                gbuf, gbn = wq[m % 4], nb(f"wq{m % 4}")
                ubuf, ubn = wu[m % 4], nb(f"wu{m % 4}")
                if m + 3 < FIT:
                    P.dma("sync", wq[(m + 3) % 4][:], c.wqg_hbm[m + 3],
                          writes=(nb(f"wq{(m + 3) % 4}"),),
                          ch=("dW", "dR")[(m + 3) % 2])
                    P.dma("sync", wu[(m + 3) % 4][:], c.wqu_hbm[m + 3],
                          writes=(nb(f"wu{(m + 3) % 4}"),),
                          ch=("dWb", "dX")[(m + 3) % 2])
                bi = m % 3
                mm_group(c, bi, [(gbuf[:, k, :], xq16[:, k, :], pb[bi][:, 0:TS],
                                  (gbn, nb("xq16"))) for k in range(8)])
                P.e("vector", "tensor_tensor", gs[:], pb[bi][:, 0:TS], gsb[:],
                    ALU.mult, reads=(f"pb{bi}", nb("gsb")), writes=(nb("gs"),),
                    tok=True)
                P.e("scalar", "activation", gs[:], gs[:], AFT.Silu,
                    reads=(nb("gs"),), writes=(nb("gs"),), tok=True)
                bi2 = 3 + m % 3
                mm_group(c, bi2, [(ubuf[:, k, :], xq16[:, k, :], pb[bi2][:, 0:TS],
                                   (ubn, nb("xq16"))) for k in range(8)])
                P.e("scalar", "activation", us[:], pb[bi2][:, 0:TS], AFT.Copy,
                    scale=c.betaS[:, 4:5],
                    reads=(f"pb{bi2}", "betaS"), writes=(nb("us"),), tok=True)
                P.e("vector", "tensor_tensor", y16[:, m, :], gs[:], us[:],
                    ALU.mult, reads=(nb("gs"), nb("us")), writes=(nb("y16"),),
                    tok=True)
                if m == 0:
                    P.e("vector", "tensor_copy", ymacc[:], y16[:, 0, :],
                        reads=(nb("y16"),), writes=(nb("ymacc"),), tok=True)
                    P.e("vector", "tensor_copy", ymn[:], y16[:, 0, :],
                        reads=(nb("y16"),), writes=(nb("ymn"),), tok=True)
                else:
                    P.e("vector", "tensor_tensor", ymacc[:], ymacc[:],
                        y16[:, m, :], ALU.max,
                        reads=(nb("ymacc"), nb("y16")), writes=(nb("ymacc"),),
                        tok=True)
                    P.e("vector", "tensor_tensor", ymn[:], ymn[:],
                        y16[:, m, :], ALU.min,
                        reads=(nb("ymn"), nb("y16")), writes=(nb("ymn"),),
                        tok=True)
            P.e("vector", "tensor_scalar", ymn[:], ymn[:], -1.0, None, ALU.mult,
                reads=(nb("ymn"),), writes=(nb("ymn"),))
            P.e("vector", "tensor_tensor", ymacc[:], ymacc[:], ymn[:], ALU.max,
                reads=(nb("ymacc"), nb("ymn")), writes=(nb("ymacc"),))
            P.e("vector", "tensor_copy", macc[:], ymacc[:],
                reads=(nb("ymacc"),), writes=(nb("macc"),), tok=True)
            for j in range(NJ):
                P.e("tensor", "transpose", pb[6][0:128, 0:128],
                    macc[:, 128 * j:128 * (j + 1)], c.identS[:],
                    reads=(nb("macc"), "identS"), writes=("pb6",), tok=True)
                P.e("vector", "tensor_reduce", ymx[:, j:j + 1].unsqueeze(1),
                    pb[6][0:128, 0:128].unsqueeze(1), AXL.X, ALU.max,
                    reads=("pb6",), writes=(nb("ymx"),), tok=True)
            P.e("vector", "tensor_tensor", ymx[:], ymx[:], sq_[:], ALU.mult,
                reads=(nb("ymx"), nb("sq")), writes=(nb("ymx"),))
            P.e("vector", "tensor_scalar", ymx[:], ymx[:], 1e-5, None, ALU.max,
                reads=(nb("ymx"),), writes=(nb("ymx"),))
            P.e("vector", "reciprocal", yfac[:], ymx[:],
                reads=(nb("ymx"),), writes=(nb("yfac"),))
            P.e("vector", "tensor_scalar", yfac[:], yfac[:], 127.0, None,
                ALU.mult, reads=(nb("yfac"),), writes=(nb("yfac"),))
            P.e("vector", "tensor_tensor", yfac[:], yfac[:], sq_[:], ALU.mult,
                reads=(nb("yfac"), nb("sq")), writes=(nb("yfac"),), tok=True)
            P.e("vector", "tensor_scalar", ysc[:], ymx[:], 1.0 / 127.0, None,
                ALU.mult, reads=(nb("ymx"),), writes=(nb("ysc"),))
            P.e("vector", "tensor_scalar", ysc[:], ysc[:], c.betaS[:, 5:6], None,
                ALU.mult, reads=(nb("ysc"), "betaS"), writes=(nb("ysc"),),
                tok=True)
            rep_tmcol(c, yrb[:], nb("yrb"), yfac[:], nb("yfac"), NJ, cast=True)
            rep_tmcol(c, ysb[:], nb("ysb"), ysc[:], nb("ysc"), NJ)

            def yquant(i):
                P.e("vector", "tensor_tensor", ymacc[:], y16[:, i, :], yrb[:],
                    ALU.mult, reads=(nb("y16"), nb("yrb")),
                    writes=(nb("ymacc"),))
                P.e("vector", "tensor_scalar", macc[:], ymacc[:], MAGIC, None,
                    ALU.add, reads=(nb("ymacc"),), writes=(nb("macc"),))
                P.e("vector", "tensor_scalar", yqi[:], macc[:], MAGIC, None,
                    ALU.subtract, reads=(nb("macc"),), writes=(nb("yqi"),))
                P.e("vector", "tensor_copy", yq16[:, i, :], yqi[:],
                    reads=(nb("yqi"), nb("yrb"), nb("ymacc"), nb("macc")),
                    writes=(nb("y16"),), tok=True)

            def dload(k):
                buf, bn = wd[k % 4], nb(f"wd{k % 4}")
                P.dma("sync", buf[:], c.wqd_hbm[k], writes=(bn,),
                      ch=("dW", "dWb", "dR", "dX")[k % 4])
            for mg in range(2):
                if mg == 0:
                    yquant(0)
                for k in range(3):
                    dload(k)
                for k in range(FIT):
                    if mg == 0 and k + 1 < FIT:
                        yquant(k + 1)
                    if k + 3 < FIT:
                        dload(k + 3)
                    buf, bn = wd[k % 4], nb(f"wd{k % 4}")
                    for mi in range(4):
                        m = 4 * mg + mi
                        P.e("tensor", "matmul", pb[mi][:, 0:TS],
                            buf[:, 128 * m:128 * (m + 1)], yq16[:, k, :],
                            start=(k == 0), stop=(k == FIT - 1),
                            reads=(bn, nb("y16")), writes=(f"pb{mi}",),
                            tok=True)
                for mi in range(4):
                    m = 4 * mg + mi
                    P.e("vector", "tensor_tensor", ymacc[:], pb[mi][:, 0:TS],
                        ysb[:], ALU.mult,
                        reads=(f"pb{mi}", nb("ysb")), writes=(nb("ymacc"),))
                    P.e("vector", "tensor_tensor", xs[:, m, :], ymacc[:],
                        xsl[:, m, :], ALU.add,
                        reads=(nb("ymacc"), xsl_nm, f"pb{mi}", nb("ysb")),
                        writes=(nb("xs"),), tok=True)
                    P.dma("sync", dst_hbm[m, :, ts:ts + TS], xs[:, m, :],
                          reads=(nb("xs"),),
                          ch="dO" if dst_hbm is c.y_out else "dW")



def phase_mamba(c):
    P, nc, pb = c.P, c.nc, c.pball
    T, TS, NS = c.T, c.TS, c.NS
    NJ = TS // 128
    with ExitStack() as ph:
        sb = lambda n, shp, dt: ph.enter_context(
            nc.sbuf_tensor(f"mb{n}", list(shp), dt))
        nb = lambda n: f"mb{n}"
        xs = sb("xs", [128, DT, TS], F32)
        stm = sb("stm", [128, 4 * (TS // 128)], F32)
        cmc = sb("cmc", [128, 2, TS], F32)
        xln = sb("xln", [128, DT, TS], BF16)
        wpan = [sb(f"wpan{i}", [128, 8, 128], BF16) for i in range(2)]
        wops = sb("wops", [128, 4, 16, 128], BF16)
        wxp = sb("wxp", [128, 16, 128], BF16)
        wdtp = sb("wdtp", [64, 16, 128], BF16)
        xcr = sb("xcr", [128, DINT, TS + 4], BF16)
        dtl = sb("dtl", [64, TS], BF16)
        Bsl = sb("Bsl", [32, TS], BF16)
        Csl = Bsl
        Brep = sb("Brep", [128, NST, TS], BF16)
        Crep = sb("Crep", [128, NST, TS], BF16)
        dtf = sb("dtf", [128, DINT, TS], BF16)
        dtxp = sb("dtxp", [128, TS], BF16)
        zsl = sb("zsl", [128, NST, TS], BF16)
        zev = sb("zev", [128, TS], BF16)
        dA = [sb(f"dA{i}", [128, NST, TS], BF16) for i in range(2)]
        hsl = sb("hsl", [128, NST, TS], BF16)
        hlast = sb("hlast", [128, DINT, NST], F32)
        sptmp = sb("sptmp", [128, TS], F32)
        convt = sptmp

        x_cv = lambda i: xcr[:, i, 4:4 + TS]

        P.dma("gpsimd", wxp[:], c.Wx_t, writes=(nb("wxp"),))
        P.dma("gpsimd", wdtp[:], c.Wdt_t, writes=(nb("wdtp"),))
        for j in range(4):
            P.dma("gpsimd", wops[:, j], c.Wout_t[j], writes=(nb("wops"),))
        P.e("vector", "memset", hlast[:], 0.0, writes=(nb("hlast"),), tok=True)
        P.e("vector", "memset", xcr[:, :, 0:4], 0.0, writes=(nb("xcr"),), tok=True)

        def pro_head(s):
            ts = s * TS
            for i in range(DT):
                P.dma("sync", xs[:, i, :], c.x2_hbm[i, :, ts:ts + TS],
                      writes=(nb("xs"),), ch="dWb")
            ln_stats(c, xs[:], nb("xs"))
            ln_scalars(c, stm[:], nb("stm"), eps=1e-5, want_mean=True)
            rep_tmcol(c, cmc[:], nb("cmc"), stm[:, 0:2 * NJ], nb("stm"), 2 * NJ)
            for i in range(DT):
                P.e("vector", "tensor_tensor", sptmp[:], xs[:, i, :],
                    cmc[:, 0, :], ALU.mult,
                    reads=(nb("xs"), nb("cmc")), writes=(nb("sptmp"),),
                    tok=(i == DT - 1))
                P.e("vector", "tensor_tensor", xln[:, i, :], sptmp[:],
                    cmc[:, 1, :], ALU.subtract,
                    reads=(nb("sptmp"), nb("cmc")),
                    writes=(nb("xln"),), tok=(i == DT - 1))

        def xz_tile(s, j, bank):
            buf, bnm = wpan[j % 2], nb(f"wpan{j % 2}")
            load_panel(c, buf, bnm, c.Win_t[j], par=j % 2)
            mm_group(c, bank, [(buf[:, k, :], xln[:, k, :], pb[bank][:, 0:TS],
                               (bnm, nb("xln"))) for k in range(8)])
            if j < 16:
                P.e("scalar", "activation", xcr[:, j, 4:4 + TS],
                    pb[bank][:, 0:TS], AFT.Copy,
                    reads=(f"pb{bank}",), writes=(nb("xcr"),), tok=True)
            else:
                P.e("scalar", "activation", zev[:], pb[bank][:, 0:TS], AFT.Copy,
                    reads=(f"pb{bank}",), writes=(nb("zev"),), tok=True)
                P.dma("sync", c.z_hbm[s % 2, j - 16], zev[:],
                      reads=(nb("zev"),), writes=("zhb",), ch="dW")

        def pro_tail(s):
            for i in range(DINT):
                P.e("vector", "tensor_scalar", convt[:], xcr[:, i, 1:1 + TS],
                    c.cwS[:, i, 0:1], c.cbS[:, i:i + 1], ALU.mult, ALU.add,
                    reads=(nb("xcr"), "cwS", "cbS"), writes=(nb("sptmp"),))
                for kk in range(1, 4):
                    P.e("vector", "scalar_tensor_tensor", convt[:],
                        xcr[:, i, 1 + kk:1 + kk + TS], c.cwS[:, i, kk:kk + 1],
                        convt[:], ALU.mult, ALU.add,
                        reads=(nb("xcr"), "cwS", nb("sptmp")),
                        writes=(nb("sptmp"),), tok=(kk == 3))
                P.e("vector", "tensor_copy", xcr[:, i, 0:4], xcr[:, i, TS:TS + 4],
                    reads=(nb("xcr"),), writes=(nb("xcr"),), tok=True)
                P.e("scalar", "activation", x_cv(i), convt[:], AFT.Silu,
                    reads=(nb("sptmp"), nb("xcr")), writes=(nb("xcr"),), tok=True)
            mm_group(c, 0, [(wxp[:, k, :], x_cv(k), pb[0][0:128, 0:TS],
                             (nb("wxp"), nb("xcr"))) for k in range(16)])
            P.e("scalar", "activation", dtl[:], pb[0][0:64, 0:TS], AFT.Copy,
                reads=("pb0",), writes=(nb("dtl"),), tok=True)
            P.e("scalar", "activation", Bsl[:], pb[0][64:96, 0:TS], AFT.Copy,
                reads=("pb0",), writes=(nb("Bsl"),), tok=True)
            replicate_row(c, Brep[:], nb("Brep"), Bsl[0:16, :], nb("Bsl"))
            P.e("scalar", "activation", Csl[:], pb[0][96:128, 0:TS], AFT.Copy,
                reads=("pb0",), writes=(nb("Bsl"),), tok=True)
            replicate_row(c, Crep[:], nb("Crep"), Csl[0:16, :], nb("Bsl"))
            for m in range(DINT):
                mm_group(c, 1, [(wdtp[:, m, :], dtl[:], pb[1][:, 0:TS],
                                  (nb("wdtp"), nb("dtl")))])
                P.e("scalar", "activation", sptmp[:], pb[1][:, 0:TS], AFT.Exp,
                    bias=c.bdtS[:, m:m + 1],
                    reads=("pb1", "bdtS"), writes=(nb("sptmp"),), tok=True)
                P.e("vector", "tensor_scalar", sptmp[:], sptmp[:], 1.0, None,
                    ALU.add, reads=(nb("sptmp"),), writes=(nb("sptmp"),),
                    tok=True)
                P.e("scalar", "activation", dtf[:, m, :], sptmp[:], AFT.Ln,
                    reads=(nb("sptmp"),), writes=(nb("dtf"),), tok=True)

        def em_z(p):
            pr = p % 2
            db, dn = dA[pr], nb(f"dA{pr}")
            for n in range(NST):
                P.e("scalar", "activation", db[:, n, :], dtf[:, p, :],
                    AFT.Exp, scale=c.A_sb[:, p, n:n + 1],
                    reads=(nb("dtf"), "A_sb"), writes=(dn,),
                    tok=(n == NST - 1))

        def em_scan(p):
            pr = p % 2
            zb, zn = zsl, nb("zsl")
            db, dn = dA[pr], nb(f"dA{pr}")
            P.e("vector", "tensor_tensor", dtxp[:], dtf[:, p, :],
                x_cv(p), ALU.mult,
                reads=(nb("dtf"), nb("xcr")), writes=(nb("dtxp"),), tok=True)
            P.e("vector", "tensor_tensor", zb[:], Brep[:],
                dtxp[:].unsqueeze(1).broadcast_to([128, NST, TS]),
                ALU.mult,
                reads=(nb("dtxp"), nb("Brep")), writes=(zn,), tok=True)
            for n in range(NST):
                P.e("vector", "tensor_tensor_scan", hsl[:, n, :],
                    db[:, n, :], zb[:, n, :], hlast[:, p, n:n + 1],
                    ALU.mult, ALU.add,
                    reads=(dn, zn, nb("hlast")),
                    writes=(nb("hsl"),), tok=(n == NST - 1))
            P.e("vector", "tensor_copy", hlast[:, p, :], hsl[:, :, TS - 1],
                reads=(nb("hsl"),), writes=(nb("hlast"),), tok=True)
            P.e("vector", "tensor_tensor",
                db[:].rearrange("p n t -> p (n t)"),
                hsl[:].rearrange("p n t -> p (n t)"),
                Crep[:].rearrange("p n t -> p (n t)"), ALU.mult,
                reads=(nb("hsl"), nb("Crep")), writes=(dn,), tok=True)
            for st in (8, 4, 2):
                P.e("vector", "tensor_tensor",
                    db[:, 0:st, :].rearrange("p n t -> p (n t)"),
                    db[:, 0:st, :].rearrange("p n t -> p (n t)"),
                    db[:, st:2 * st, :].rearrange("p n t -> p (n t)"),
                    ALU.add, reads=(dn,), writes=(dn,),
                    tok=(st == 2))
            P.e("vector", "tensor_tensor", dtf[:, p, :], db[:, 0, :],
                db[:, 1, :], ALU.add,
                reads=(dn,), writes=(nb("dtf"),), tok=True)
            P.e("vector", "scalar_tensor_tensor", dtf[:, p, :],
                x_cv(p), c.dpS[:, p:p + 1], dtf[:, p, :],
                ALU.mult, ALU.add,
                reads=(nb("xcr"), "dpS", nb("dtf")), writes=(nb("dtf"),),
                tok=True)

        def post_tail(s, last):
            ts = s * TS
            for i in range(DINT):
                P.dma("sync", zsl[:, i, :], c.z_hbm[s % 2, i],
                      reads=("zhb",), writes=(nb("zsl"),), ch="dWb")
            zf = zsl[:].rearrange("p d t -> p (d t)")
            P.e("scalar", "activation", zf, zf, AFT.Silu,
                reads=(nb("zsl"),), writes=(nb("zsl"),), tok=True)
            for i in range(DINT):
                P.e("vector", "tensor_tensor", dtf[:, i, :], dtf[:, i, :],
                    zsl[:, i, :], ALU.mult,
                    reads=(nb("dtf"), nb("zsl")), writes=(nb("dtf"),),
                    tok=True)
            for m in range(DT):
                if m >= 4:
                    P.dma("gpsimd", wops[:, m - 4], c.Wout_t[m],
                          writes=(nb("wops"),))
                bank = 2 + (m % 4)
                slot = m % 4
                mm_group(c, bank, [(wops[:, slot, k, :], dtf[:, k, :],
                                    pb[bank][:, 0:TS], (nb("wops"), nb("dtf")))
                                   for k in range(16)])
                P.e("vector", "tensor_copy", hsl[:, m, :], pb[bank][:, 0:TS],
                    reads=(f"pb{bank}",), writes=(nb("hsl"),), tok=True)
                P.dma("sync", c.x3_hbm[m, :, ts:ts + TS], hsl[:, m, :],
                      reads=(nb("hsl"),), ch="dW")
                if m >= 4 and not last:
                    P.dma("gpsimd", wops[:, m - 4], c.Wout_t[m - 4],
                          writes=(nb("wops"),))

        pro_head(0)
        for j in range(FIT):
            xz_tile(0, j, j % 2)
        pro_tail(0)
        for s in range(NS):
            if s + 1 < NS:
                pro_head(s + 1)
            em_z(0)
            for p in range(DINT):
                if p + 1 < DINT:
                    em_z(p + 1)
                em_scan(p)
                if s + 1 < NS:
                    xz_tile(s + 1, p, 0)
                    xz_tile(s + 1, 16 + p, 1)
            post_tail(s, last=(s + 1 == NS))
            if s + 1 < NS:
                pro_tail(s + 1)



def kernel(**inputs):
    shared = host_prepare(inputs)
    x = np.ascontiguousarray(np.asarray(inputs["L0_attn_a_hidden_states"],
                                        dtype=np.float32))
    Bn, T, _ = x.shape
    nc = build_program(T=T)
    in_maps = []
    for b in range(Bn):
        m = dict(shared)
        m["x_in"] = np.ascontiguousarray(x[b].T.reshape(DT, 128, T))
        in_maps.append(m)
    res = run_bass_kernel_spmd(nc, in_maps, core_ids=list(range(Bn)))
    out = np.stack([r["y_out"].reshape(D, T).T for r in res.results])
    return np.ascontiguousarray(out.astype(np.float32))


# version 24
# speedup vs baseline: 1.0121x; 1.0121x over previous
from contextlib import ExitStack

import numpy as np

import concourse.bass as bass
import concourse.mybir as mybir
from concourse.bass_utils import run_bass_kernel_spmd

F32 = mybir.dt.float32
BF16 = mybir.dt.bfloat16
F16 = mybir.dt.float16
I16 = mybir.dt.int16
ALU = mybir.AluOpType
AFT = mybir.ActivationFunctionType
AXL = mybir.AxisListType

MAGIC = float(np.float32(1.5 * 2 ** 23))
D = 1024
DT = 8
H = 4
DV = 256
CH = 64
MIT = 22
FIT = 32
DINT = 16
NST = 16
GLN = 16.0

ENGS = ("sync", "tensor", "vector", "scalar", "gpsimd")
ECH = {"sync": "cy", "tensor": "mm", "vector": "vx", "scalar": "sx", "gpsimd": "gx"}


class Prog:

    def __init__(self, nc, blk, stack):
        self.nc, self.blk, self.stack = nc, blk, stack
        self.q = {e: [] for e in ENGS}
        self.chan = {}
        self.last_wait = {e: {} for e in ENGS}
        self.dirty = {e: set() for e in ENGS}
        self.dirtyr = {e: set() for e in ENGS}
        self.wr = {}
        self.rd = {}
        self.untok = {e: 0 for e in ENGS}

    def mkchan(self, name):
        h = self.stack.enter_context(self.nc.semaphore(name))
        self.chan[name] = [h, 0]

    def cur(self, ch):
        return self.chan[ch][1]

    def wait_tok(self, eng, tok):
        ch, val = tok
        if val <= self.last_wait[eng].get(ch, 0):
            return
        h = self.chan[ch][0]
        self.q[eng].append(("w", h, val))
        self.last_wait[eng][ch] = val

    def e(self, eng, meth, *args, reads=(), writes=(), tok=False, dma_ch=None,
          **kwargs):
        reads, writes = tuple(reads), tuple(writes)
        rec_eng = f"@{dma_ch}" if dma_ch else eng
        for nm in reads:
            w = self.wr.get(nm)
            if w is not None and w[0] != rec_eng:
                if w[1] is None:
                    raise RuntimeError(f"RAW {nm}: writer on {w[0]} lacks token")
                self.wait_tok(eng, w[1])
        for nm in writes:
            w = self.wr.get(nm)
            if w is not None and w[0] != rec_eng:
                if w[1] is None:
                    raise RuntimeError(f"WAW {nm}: writer on {w[0]} lacks token")
                self.wait_tok(eng, w[1])
            last_per_eng = {}
            for (re, rt) in self.rd.get(nm, ()):
                if re != rec_eng:
                    last_per_eng[re] = rt
            for re, rt in last_per_eng.items():
                if rt is None:
                    raise RuntimeError(f"WAR {nm}: last reader on {re} lacks token")
                self.wait_tok(eng, rt)
        if not dma_ch:
            if (((set(reads) | set(writes)) & self.dirty[eng])
                    or (set(writes) & self.dirtyr[eng])):
                self.q[eng].append(("d",))
                self.dirty[eng] = set()
                self.dirtyr[eng] = set()
            self.dirty[eng] |= set(writes)
            self.dirtyr[eng] |= set(reads)
        token = None
        if tok or dma_ch:
            ch = dma_ch if dma_ch else ECH[eng]
            cc = self.chan[ch]
            cc[1] += 16 if dma_ch else 1
            token = (ch, cc[1])
            self.q[eng].append(("i", meth, args, kwargs, cc[0], 16 if dma_ch else 1))
            self.untok[eng] = 0
        else:
            self.q[eng].append(("o", meth, args, kwargs))
            if meth != "nop":
                self.untok[eng] += 1
        for nm in writes:
            self.wr[nm] = (rec_eng, token)
            self.rd[nm] = []
        for nm in reads:
            self.rd.setdefault(nm, []).append((rec_eng, token))
        return token

    def dma(self, eng, out_ap, in_ap, reads=(), writes=(), ch=None):
        ch = ch or ("dG" if eng == "gpsimd" else "dW")
        self.wait_tok(eng, (ch, self.chan[ch][1]))
        return self.e(eng, "dma_start", out_ap, in_ap,
                      reads=reads, writes=writes, dma_ch=ch)

    def barrier(self):
        for en in ENGS:
            assert self.untok[en] == 0, (
                f"barrier: engine {en} has {self.untok[en]} untokened trailing "
                f"ops; give the last one tok=True")
        marks = [("mm", self.cur("mm")), ("vx", self.cur("vx")),
                 ("sx", self.cur("sx")), ("gx", self.cur("gx")),
                 ("cy", self.cur("cy")),
                 ("dW", self.cur("dW")), ("dWb", self.cur("dWb")),
                 ("dG", self.cur("dG")), ("dGb", self.cur("dGb")),
                 ("dR", self.cur("dR")), ("dO", self.cur("dO"))]
        for en in ENGS:
            for (ch, v) in marks:
                if v:
                    self.wait_tok(en, (ch, v))
        self.wr = {}
        self.rd = {}

    def flush(self):
        for en in ENGS:
            ops = self.q[en]
            self.q[en] = []
            if not ops:
                continue

            def body(eng, ops=ops):
                for it in ops:
                    try:
                        if it[0] == "w":
                            eng.wait_ge(it[1], it[2])
                        elif it[0] == "d":
                            eng.drain()
                        elif it[0] == "o":
                            getattr(eng, it[1])(*it[2], **it[3])
                        else:
                            getattr(eng, it[1])(*it[2], **it[3]).then_inc(it[4], it[5])
                    except Exception as ex:
                        raise RuntimeError(
                            f"emit failed: {it[1] if it[0] in 'oi' else it[0]} "
                            f"args={[getattr(a, 'shape', a) for a in (it[2] if it[0] in 'oi' else ())]}"
                        ) from ex
            getattr(self.blk, en)(body)



def _panels(w, mt=128):
    K, M = w.shape
    return np.ascontiguousarray(
        w.reshape(K // 128, 128, M // mt, mt).transpose(2, 1, 0, 3))


def _kslabs(w):
    K, N = w.shape
    return np.ascontiguousarray(w.reshape(K // 128, 128, N))


def _kslabs_pm(w, halves=2):
    K, N = w.shape
    return np.ascontiguousarray(
        w.reshape(K // 128, 128, halves, N // halves).transpose(2, 1, 0, 3))


def _pcol(v):
    return np.ascontiguousarray(np.asarray(v).reshape(-1, 128).T)


def host_prepare(inp):
    f = lambda a: np.ascontiguousarray(np.asarray(a), dtype=np.float32)
    d = {}
    d["Wq_t"] = _panels(f(inp["L0_attn_a_Wq"]))
    d["Wk_t"] = _panels(f(inp["L0_attn_a_Wk"]))
    d["Wv_t"] = _kslabs_pm(f(inp["L0_attn_a_Wv"]))
    d["Wg_t"] = _kslabs_pm(f(inp["L0_attn_a_Wg"]))
    d["Wgk1_t"] = np.ascontiguousarray(
        f(inp["L0_attn_a_Wgk1"]).reshape(8, 128, 16).transpose(1, 0, 2))
    d["Wgk2_t"] = np.ascontiguousarray(
        f(inp["L0_attn_a_Wgk2"]).reshape(16, 4, 128).transpose(1, 0, 2))
    d["bgk2_c"] = _pcol(f(inp["L0_attn_a_bgk2"]))
    d["gw_c"] = _pcol(f(inp["L0_attn_a_gnorm_w"]))
    d["Wo_t"] = _kslabs_pm(f(inp["L0_attn_a_Wo"]))
    d["anw_c"] = _pcol(f(inp["L0_attn_a_attn_norm_w"]))
    d["mnw_c"] = _pcol(f(inp["L0_attn_a_mlp_norm_w"]))
    d["Wmg_t"] = _panels(f(inp["L0_attn_a_W_gate"]))
    d["Wmu_t"] = _panels(f(inp["L0_attn_a_W_up"]))
    d["Wmd_t"] = _panels(f(inp["L0_attn_a_W_down"]))
    d["fg_t"] = _panels(f(inp["L0_ffn_w_gate"]).T)
    d["fu_t"] = _panels(f(inp["L0_ffn_w_up"]).T)
    d["fd_t"] = _kslabs(f(inp["L0_ffn_w_down"]).T)
    d["Win_t"] = _panels(f(inp["L0_attn_b_W_in"]).T)
    d["cw_c"] = np.ascontiguousarray(
        f(inp["L0_attn_b_conv_w"]).reshape(16, 128, 4).transpose(1, 0, 2))
    d["cb_c"] = _pcol(f(inp["L0_attn_b_conv_b"]))
    wx = f(inp["L0_attn_b_W_x"])
    wxp_ = np.zeros((128, 2048), np.float32)
    wxp_[0:64] = wx[0:64]
    wxp_[64:80] = wx[64:80]
    wxp_[96:112] = wx[80:96]
    d["Wx_t"] = np.ascontiguousarray(
        wxp_.T.reshape(16, 128, 128).transpose(1, 0, 2))
    d["Wdt_t"] = np.ascontiguousarray(
        f(inp["L0_attn_b_W_dt"]).T.reshape(64, 16, 128))
    d["bdt_c"] = _pcol(f(inp["L0_attn_b_b_dt"]))
    d["alog_t"] = np.ascontiguousarray(
        f(inp["L0_attn_b_A_log"]).reshape(16, 128, 16).transpose(1, 0, 2))
    d["dp_c"] = _pcol(f(inp["L0_attn_b_D_param"]))
    d["Wout_t"] = _panels(f(inp["L0_attn_b_W_out"]).T)
    d["ident"] = np.eye(128, dtype=np.float32)
    d["onecol"] = np.ones((128, 1), dtype=np.float32)
    gm = np.ones((1, 512), dtype=np.float32)
    gm[0, ::CH] = 0.0
    d["gmask"] = gm
    si = np.arange(CH)
    d["smask"] = (si[:, None] <= si[None, :]).astype(np.float32)
    return d


DEBUG_OUT = False


class Ctx:
    pass



def mm_group(c, bank, mms, tile_position=None):
    P = c.P
    n = len(mms)
    union = []
    for (_, _, _, reads) in mms:
        for r in reads:
            if r not in union:
                union.append(r)
    for j, (lh, rh, out, reads) in enumerate(mms):
        kw = {}
        if tile_position is not None:
            kw["tile_position"] = tile_position
        rd = tuple(union) if j == n - 1 else tuple(reads)
        P.e("tensor", "matmul", out, lh, rh, start=(j == 0), stop=(j == n - 1),
            reads=rd, writes=(f"pb{bank}",), tok=(j == n - 1), **kw)


def ln_stats(c, src_ap, src_nm, want_mean=True):
    P = c.P
    TS = c.TS
    NJ = TS // 128
    NCHK = 2 if NJ >= 2 else 1
    HH = TS // NCHK
    nmm = NCHK * (NJ // NCHK) * (2 if want_mean else 1) * DT
    imm = 0
    for hh in range(NCHK):
        sl = src_ap[:, :, HH * hh:HH * hh + HH]
        if want_mean:
            P.e("scalar", "activation", c.stat_xb[:], sl, AFT.Copy,
                reads=(src_nm,), writes=("stat_xb",), tok=True)
        P.e("scalar", "activation", c.stat_sq[:], sl, AFT.Square,
            reads=(src_nm,), writes=("stat_sq",), tok=True)
        for jj in range(NJ // NCHK):
            j = hh * (NJ // NCHK) + jj
            srcs = ([(c.stat_xb, "stat_xb", j), (c.stat_sq, "stat_sq", NJ + j)]
                    if want_mean else [(c.stat_sq, "stat_sq", NJ + j)])
            for (buf, bnm, col) in srcs:
                for k in range(DT):
                    imm += 1
                    P.e("tensor", "matmul", c.pball[6][0:128, col:col + 1],
                        buf[:, k, 128 * jj:128 * (jj + 1)], c.onecolS[:],
                        start=(imm == 1), stop=(imm == nmm),
                        reads=(bnm, "onecolS"), writes=("pb6",),
                        tok=(k == DT - 1))


def ln_scalars(c, stm, stm_nm, eps=1e-5, want_mean=True, double_rms=False):
    P = c.P
    NJ = c.TS // 128
    w0 = stm[:, 0:NJ]
    w1 = stm[:, NJ:2 * NJ]
    w2 = stm[:, 2 * NJ:3 * NJ]
    w3 = stm[:, 3 * NJ:4 * NJ]
    if want_mean:
        P.e("scalar", "activation", w3, c.pball[6][0:128, 0:NJ], AFT.Copy,
            scale=1.0 / D, reads=("pb6",), writes=(stm_nm,), tok=True)
        P.e("scalar", "activation", w0, c.pball[6][0:128, NJ:2 * NJ], AFT.Copy,
            scale=1.0 / D, reads=("pb6",), writes=(stm_nm,), tok=True)
        P.e("vector", "tensor_tensor", w2, w3, w3, ALU.mult,
            reads=(stm_nm,), writes=(stm_nm,))
        P.e("vector", "tensor_tensor", w0, w0, w2, ALU.subtract,
            reads=(stm_nm,), writes=(stm_nm,))
    else:
        P.e("scalar", "activation", w0, c.pball[6][0:128, NJ:2 * NJ], AFT.Copy,
            scale=1.0 / D, reads=("pb6",), writes=(stm_nm,), tok=True)
    if double_rms:
        P.e("vector", "tensor_scalar", w2, w0, 1.0 + 1e-6, 1e-11,
            ALU.mult, ALU.add, reads=(stm_nm,), writes=(stm_nm,))
    P.e("vector", "tensor_scalar", w0, w0, 1.0, eps, ALU.mult, ALU.add,
        reads=(stm_nm,), writes=(stm_nm,), tok=True)
    P.e("scalar", "activation", w0, w0, AFT.Ln,
        reads=(stm_nm,), writes=(stm_nm,), tok=True)
    P.e("scalar", "activation", w0, w0, AFT.Exp, scale=-0.5,
        reads=(stm_nm,), writes=(stm_nm,), tok=True)
    if double_rms:
        P.e("scalar", "activation", w2, w2, AFT.Ln,
            reads=(stm_nm,), writes=(stm_nm,), tok=True)
        P.e("scalar", "activation", w2, w2, AFT.Exp, scale=-0.5,
            reads=(stm_nm,), writes=(stm_nm,), tok=True)
    if want_mean:
        P.e("vector", "tensor_tensor", w1, w3, w0, ALU.mult,
            reads=(stm_nm,), writes=(stm_nm,), tok=True)
    if double_rms:
        P.e("vector", "reciprocal", w3, w0, reads=(stm_nm,), writes=(stm_nm,))
        P.e("vector", "tensor_tensor", w2, w2, w3, ALU.mult,
            reads=(stm_nm,), writes=(stm_nm,), tok=True)


def replicate_row(c, dst_ap, dst_nm, row_ap, row_nm, cast=False):
    P = c.P
    shp = row_ap.shape
    n = int(np.prod(shp))
    slot = c.rep_slot
    c.rep_slot = (c.rep_slot + 1) % c.rep_nslot
    hbm = c.rows_hbm if row_ap.dtype == F32 else c.rowsb_hbm
    scr = hbm[slot:slot + 1, 0:n]
    snm = f"rowh{slot}_{row_ap.dtype}"
    if len(shp) == 2 and shp[0] > 1:
        scr_w = hbm[slot, 0:n].rearrange("(a b) -> a b", b=shp[1])
    else:
        scr_w = hbm[slot, 0:n].rearrange("t -> () t")
    P.dma("scalar", scr_w, row_ap, reads=(row_nm,), writes=(snm,), ch="dR")
    eng = "gpsimd" if cast else "scalar"
    ch = "dG" if cast else "dR"
    return P.dma(eng, dst_ap, scr.partition_broadcast(128),
                 reads=(snm,), writes=(dst_nm,), ch=ch)


def rep_tmcol(c, dst_ap, dst_nm, col_ap, col_nm, nj, cast=False):
    P = c.P
    P.e("tensor", "transpose", c.pball[6][0:nj, 0:128], col_ap, c.identS[:],
        reads=(col_nm, "identS"), writes=("pb6",), tok=True)
    P.e("scalar", "activation", c.rtmpS[0:nj, :], c.pball[6][0:nj, 0:128],
        AFT.Copy, reads=("pb6",), writes=("rtmpS",), tok=True)
    return replicate_row(c, dst_ap, dst_nm, c.rtmpS[0:nj, :], "rtmpS", cast=cast)


def load_panel(c, buf, buf_nm, src_ap, kt=8, par=0):
    return c.P.dma("gpsimd", buf[:, 0:kt, :], src_ap,
                   writes=(buf_nm,), ch=("dG", "dGb")[par])



def build_program(T=2048):
    c = Ctx()
    c.T = T
    c.TS = min(512, T)
    c.NS = T // c.TS
    c.NCH = c.TS // CH
    nc = bass.Bass("TRN2", target_bir_lowering=False, debug=False)
    c.nc = nc

    di = lambda n, shp: nc.dram_tensor(n, list(shp), F32, kind="ExternalInput").ap()
    c.x_in = di("x_in", [DT, 128, T])
    for n, shp in [
        ("Wq_t", [4, 128, 8, 128]), ("Wk_t", [4, 128, 8, 128]),
        ("Wv_t", [2, 128, 8, 512]), ("Wg_t", [2, 128, 8, 512]),
        ("Wgk1_t", [128, 8, 16]), ("Wgk2_t", [4, 16, 128]),
        ("bgk2_c", [128, 4]), ("gw_c", [128, 2]), ("Wo_t", [2, 128, 8, 512]),
        ("anw_c", [128, 8]), ("mnw_c", [128, 8]),
        ("Wmg_t", [MIT, 128, 8, 128]), ("Wmu_t", [MIT, 128, 8, 128]),
        ("Wmd_t", [DT, 128, MIT, 128]),
        ("fg_t", [FIT, 128, 8, 128]), ("fu_t", [FIT, 128, 8, 128]),
        ("fd_t", [FIT, 128, 1024]),
        ("Win_t", [FIT, 128, 8, 128]),
        ("cw_c", [128, 16, 4]), ("cb_c", [128, 16]),
        ("Wx_t", [128, 16, 128]), ("Wdt_t", [64, 16, 128]),
        ("bdt_c", [128, 16]), ("alog_t", [128, 16, 16]), ("dp_c", [128, 16]),
        ("Wout_t", [DT, 128, 16, 128]),
        ("ident", [128, 128]), ("onecol", [128, 1]),
        ("gmask", [1, 512]), ("smask", [CH, CH]),
    ]:
        setattr(c, n, di(n, shp))
    c.y_out = nc.dram_tensor("y_out", [DT, 128, T], F32, kind="ExternalOutput").ap()

    ikind = "ExternalOutput" if DEBUG_OUT else "Internal"
    dsc = lambda n, shp, dt=F32, kind="Internal": nc.dram_tensor(
        n, list(shp), dt, kind=kind).ap()
    c.x2_hbm = dsc("x2_hbm", [DT, 128, T], kind=ikind)
    c.x3_hbm = dsc("x3_hbm", [DT, 128, T], BF16, kind=ikind)
    c.wqg_hbm = dsc("wqg_hbm", [FIT, 128, 8, 128], F16, kind=ikind)
    c.wqu_hbm = dsc("wqu_hbm", [FIT, 128, 8, 128], F16)
    c.wqd_hbm = dsc("wqd_hbm", [FIT, 128, 1024], F16, kind=ikind)
    c.rep_nslot = 8
    c.rep_slot = 0
    c.rows_hbm = dsc("rows_hbm", [c.rep_nslot, NST * c.TS], F32)
    c.z_hbm = dsc("z_hbm", [2, DINT, 128, c.TS], BF16)
    c.rowsb_hbm = dsc("rowsb_hbm", [c.rep_nslot, NST * c.TS], BF16)

    stack = ExitStack()
    c.stack = stack
    c.sb = lambda n, shp, dt: stack.enter_context(nc.sbuf_tensor(n, list(shp), dt))
    c.pball = [stack.enter_context(nc.psum_tensor(f"pb{i}", [128, 512], F32))
               for i in range(7)]
    c.pbT = stack.enter_context(nc.psum_tensor("pbT", [128, 1024], BF16))

    x0_scope = ExitStack()
    c.x0 = x0_scope.enter_context(
        nc.sbuf_tensor("x0", [128, DT, T], F32, side="right"))
    c.identS = c.sb("identS", [128, 128], F32)
    c.identB = c.sb("identB", [128, 128], BF16)
    c.onecolS = c.sb("onecolS", [128, 1], BF16)
    c.onecolF = c.sb("onecolF", [128, 1], F32)
    c.gmaskS = c.sb("gmaskS", [128, c.TS], F32)
    c.smaskS = c.sb("smaskS", [CH, CH], F32)
    c.anwS = c.sb("anwS", [128, 8], F32)
    c.mnwS = c.sb("mnwS", [128, 8], F32)
    c.gwS = c.sb("gwS", [128, 2], F32)
    c.nbgS = c.sb("nbgS", [128, 4], F32)
    c.bdtS = c.sb("bdtS", [128, 16], F32)
    c.dpS = c.sb("dpS", [128, 16], F32)
    c.cwS = c.sb("cwS", [128, 16, 4], F32)
    c.cbS = c.sb("cbS", [128, 16], F32)
    c.A_sb = c.sb("A_sb", [128, 16, 16], F32)
    c.betaS = c.sb("betaS", [128, 8], F32)
    nchk = 2 if c.TS // 128 >= 2 else 1
    c.stat_xb = c.sb("stat_xb", [128, DT, c.TS // nchk], BF16)
    c.stat_sq = c.sb("stat_sq", [128, DT, c.TS // nchk], BF16)
    c.rtmpS = c.sb("rtmpS", [16, 128], F32)

    with nc.Block() as blk:
        P = Prog(nc, blk, stack)
        c.P = P
        for ch in ["cy", "mm", "vx", "sx", "gx", "dW", "dWb", "dG", "dGb",
                   "dR", "dO", "dX"]:
            P.mkchan(ch)

        phase_load(c)
        P.barrier(); P.flush()
        phase_attn(c)
        P.barrier(); P.flush()
        ws = ExitStack()
        wq_units = build_wq_units(c, ws)
        phase_mlp(c, wq_units=wq_units)
        P.barrier(); P.flush()
        ws.close()
        phase_ffn(c, src_hbm=None, dst_hbm=c.x2_hbm, tag="f1")
        P.barrier(); P.flush()
        x0_scope.close()
        phase_mamba(c)
        P.barrier(); P.flush()
        phase_ffn(c, src_hbm=c.x2_hbm, dst_hbm=c.y_out, tag="f2",
                  add_hbm=c.x3_hbm)
        P.wait_tok("sync", ("dO", P.cur("dO")))
        P.flush()

    stack.close()
    return nc


def phase_load(c):
    P = c.P
    for i in range(DT):
        P.dma("sync", c.x0[:, i, :], c.x_in[i], writes=("x0",))
    P.dma("sync", c.identS[:], c.ident, writes=("identS",))
    P.dma("sync", c.smaskS[:], c.smask, writes=("smaskS",))
    P.dma("sync", c.gmaskS[:], c.gmask[0:1, 0:c.TS].partition_broadcast(128),
          writes=("gmaskS",))
    for (dst, nm, src) in [(c.anwS, "anwS", c.anw_c), (c.mnwS, "mnwS", c.mnw_c),
                           (c.gwS, "gwS", c.gw_c), (c.nbgS, "nbgS", c.bgk2_c),
                           (c.bdtS, "bdtS", c.bdt_c), (c.dpS, "dpS", c.dp_c),
                           (c.cwS, "cwS", c.cw_c), (c.cbS, "cbS", c.cb_c)]:
        P.dma("sync", dst[:], src, writes=(nm,))
    P.dma("gpsimd", c.onecolS[:], c.onecol, writes=("onecolS",))
    P.dma("gpsimd", c.identB[:], c.ident, writes=("identB",))
    P.dma("sync", c.onecolF[:], c.onecol, writes=("onecolF",))
    alog = c.sb("alogS", [128, 16, 16], F32)
    P.dma("sync", alog[:], c.alog_t, writes=("alogS",))
    P.e("scalar", "activation", alog[:].rearrange("p a b -> p (a b)"),
        alog[:].rearrange("p a b -> p (a b)"), AFT.Exp,
        reads=("alogS",), writes=("alogS",), tok=True)
    P.e("vector", "tensor_scalar", c.A_sb[:].rearrange("p a b -> p (a b)"),
        alog[:].rearrange("p a b -> p (a b)"), -1.0, None, ALU.mult,
        reads=("alogS",), writes=("A_sb",), tok=True)
    P.e("vector", "tensor_scalar", c.nbgS[:], c.nbgS[:], -1.0, None, ALU.mult,
        reads=("nbgS",), writes=("nbgS",), tok=True)



def phase_attn(c):
    P, nc, pb = c.P, c.nc, c.pball
    T, TS, NS, NCH = c.T, c.TS, c.NS, c.NCH
    scale_dk = float(128 ** -0.5)

    with ExitStack() as ph:
        sb = lambda n, shp, dt: ph.enter_context(nc.sbuf_tensor(n, list(shp), dt))
        stm = sb("stm", [128, 4 * (TS // 128)], F32)
        cmc = sb("cmc", [128, 3, TS], F32)
        xln = sb("xln", [128, DT, TS], BF16)
        h_bf = sb("h_bf", [128, DT, TS], BF16)
        wpan = [sb(f"wpan{i}", [128, 8, 128], BF16) for i in range(2)]
        wgk2 = sb("wgk2", [16, 4, 128], BF16)
        wvg = sb("wvg", [128, 8, 512], BF16)
        qf = sb("qf", [128, 4, TS], BF16)
        kf = sb("kf", [128, 4, TS], BF16)
        spb = sb("spb", [128, 4, TS], BF16)
        t1b = sb("t1b", [16, TS], BF16)
        v_tm = sb("v_tm", [64, NCH, 1024], BF16)
        g_tm = sb("g_tm", [64, NCH, 1024], BF16)
        cum = sb("cum", [128, TS], F32)
        ebp = sb("ebp", [128, TS], F32)
        ebn = sb("ebn", [128, TS], F32)
        qe = [sb(f"qe{i}", [128, TS], BF16) for i in range(2)]
        ke_bf = [sb(f"ke_bf{i}", [128, TS], BF16) for i in range(2)]
        ebl = [sb(f"ebl{i}", [128, NCH], F32) for i in range(2)]
        am = [sb(f"am{i}", [64, CH], BF16) for i in range(2)]
        kdtm = [sb(f"kdtm{i}", [64, 128], BF16) for i in range(2)]
        sst = [sb(f"sst{h}", [128, DV], F32) for h in range(H)]
        sstb = [sb(f"sstb{h}", [128, DV], BF16) for h in range(H)]
        sst_t = sb("sst_t", [128, DV], F32)
        o_tm = sb("o_tm", [64, NCH, 1024], BF16)
        osq = sb("osq", [64, NCH // 4 if NCH >= 4 else NCH, 1024], BF16)
        ssq = sb("ssq", [64, NCH, 4], F32)
        ogT = sb("ogT", [128, 8, TS], BF16)
        x2t = sb("x2t", [128, TS], F32)

        for s in range(NS):
            ts = s * TS
            x0sl = c.x0[:, :, ts:ts + TS]

            NJ = TS // 128
            ln_stats(c, x0sl, "x0")
            ln_scalars(c, stm[:], "stm", eps=1e-5, want_mean=True,
                       double_rms=True)
            rep_tmcol(c, cmc[:], "cmc", stm[:, 0:3 * NJ], "stm", 3 * NJ)
            for i in range(DT):
                P.e("vector", "scalar_tensor_tensor", x2t[:],
                    c.x0[:, i, ts:ts + TS], 1.0, cmc[:, 0, :], ALU.mult, ALU.mult,
                    reads=("x0", "cmc"), writes=("x2t",))
                P.e("vector", "tensor_tensor", xln[:, i, :], x2t[:],
                    cmc[:, 1, :], ALU.subtract,
                    reads=("x2t", "cmc"), writes=("xln",),
                    tok=(i == DT - 1))
                P.e("vector", "scalar_tensor_tensor", h_bf[:, i, :], xln[:, i, :],
                    c.anwS[:, i:i + 1], cmc[:, 2, :], ALU.mult, ALU.mult,
                    reads=("xln", "anwS", "cmc"), writes=("h_bf",),
                    tok=(i == DT - 1))

            P.dma("gpsimd", wpan[0][:, 0:8, 0:16], c.Wgk1_t,
                  writes=("wpan0",))
            mm_group(c, 0, [(wpan[0][:, k, 0:16], h_bf[:, k, :],
                             pb[0][0:16, 0:TS], ("wpan0", "h_bf"))
                            for k in range(8)])
            P.e("scalar", "activation", t1b[:], pb[0][0:16, 0:TS], AFT.Copy,
                reads=("pb0",), writes=("t1b",), tok=True)
            P.dma("gpsimd", wgk2[:], c.Wgk2_t.rearrange("a p m -> p a m"),
                  writes=("wgk2",))
            for m in range(4):
                mm_group(c, 1, [(wgk2[0:16, m, :], t1b[:], pb[1][:, 0:TS],
                                 ("wgk2", "t1b"))])
                P.e("scalar", "activation", spb[:, m, :], pb[1][:, 0:TS], AFT.Exp,
                    scale=-1.0, bias=c.nbgS[:, m:m + 1],
                    reads=("pb1", "nbgS"), writes=("spb",), tok=True)
            spf = spb[:].rearrange("p a t -> p (a t)")
            P.e("vector", "tensor_scalar", spf, spf, 1.0, None, ALU.add,
                reads=("spb",), writes=("spb",), tok=True)
            P.e("scalar", "activation", spf, spf, AFT.Ln,
                reads=("spb",), writes=("spb",), tok=True)

            for (wt, dst, nm) in ((c.Wq_t, qf, "qf"), (c.Wk_t, kf, "kf")):
                for m in range(4):
                    buf, bn = wpan[m % 2], f"wpan{m % 2}"
                    load_panel(c, buf, bn, wt[m], par=m % 2)
                    bi = 2 + (m % 2)
                    mm_group(c, bi, [(buf[:, k, :], h_bf[:, k, :],
                                      pb[bi][:, 0:TS], (bn, "h_bf"))
                                     for k in range(8)])
                    P.e("scalar", "activation", dst[:, m, :], pb[bi][:, 0:TS],
                        AFT.Copy, reads=(f"pb{bi}",), writes=(nm,), tok=True)

            for (wt, dst, nm) in ((c.Wv_t, v_tm, "v_tm"), (c.Wg_t, g_tm, "g_tm")):
                for half in range(2):
                    P.dma("gpsimd", wvg[:], wt[half], writes=("wvg",))
                    for j in range(TS // 128):
                        bi = 4 + half
                        mm_group(c, bi, [(h_bf[:, k, 128 * j:128 * (j + 1)],
                                          wvg[:, k, :],
                                          pb[bi][:, 0:512], ("h_bf", "wvg"))
                                         for k in range(8)])
                        for q in range(2):
                            cidx = 2 * j + q
                            P.e("scalar", "activation",
                                dst[0:64, cidx, 512 * half:512 * (half + 1)],
                                pb[bi][64 * q:64 * q + 64, 0:512], AFT.Copy,
                                reads=(f"pb{bi}",), writes=(nm,), tok=True)

            if s == 0:
                for h in range(H):
                    P.e("vector", "memset", sst[h][:], 0.0,
                        writes=(f"sst{h}",), tok=True)
                    P.e("vector", "memset", sstb[h][:], 0.0,
                        writes=(f"sstb{h}",), tok=True)
            cumv = cum[:].rearrange("p (a q) -> p a q", q=CH)
            for h in range(H):
                hb = h % 2
                qeh, keh = qe[hb], ke_bf[hb]
                eblh, amh, kdh = ebl[hb], am[hb], kdtm[hb]
                qn, kn = f"qe{hb}", f"ke_bf{hb}"
                en, an, dn = f"ebl{hb}", f"am{hb}", f"kdtm{hb}"
                P.e("vector", "tensor_tensor_scan", cum[:], c.gmaskS[:],
                    spb[:, h, :], 0.0, ALU.mult, ALU.add,
                    reads=("gmaskS", "spb"), writes=("cum",), tok=True)
                P.e("scalar", "activation", ebp[:], cum[:], AFT.Exp,
                    scale=-1.0 / GLN, reads=("cum",), writes=("ebp",), tok=True)
                P.e("scalar", "activation", ebn[:], cum[:], AFT.Exp,
                    scale=1.0 / GLN, reads=("cum",), writes=("ebn",), tok=True)
                P.e("scalar", "activation", eblh[:, 0:NCH], cumv[:, :, CH - 1],
                    AFT.Exp, scale=-1.0 / GLN,
                    reads=("cum",), writes=(en,), tok=True)
                P.e("vector", "scalar_tensor_tensor", qeh[:], qf[:, h, :],
                    scale_dk, ebp[:], ALU.mult, ALU.mult,
                    reads=("qf", "ebp"), writes=(qn,), tok=True)
                P.e("vector", "tensor_tensor", keh[:], kf[:, h, :], ebn[:],
                    ALU.mult, reads=("kf", "ebn"), writes=(kn,), tok=True)
                for ci in range(NCH):
                    t0 = ci * CH
                    mm_group(c, 6, [(keh[:, t0:t0 + CH], qeh[:, t0:t0 + CH],
                                     pb[6][0:64, 0:CH], (kn, qn))])
                    P.e("tensor", "transpose", c.pbT[0:64, 0:128],
                        keh[:, t0:t0 + CH], c.identB[:],
                        reads=(kn, "identB"), writes=("pbT",), tok=True)
                    P.e("vector", "tensor_tensor", amh[:], pb[6][0:64, 0:CH],
                        c.smaskS[:], ALU.mult,
                        reads=("pb6", "smaskS"), writes=(an,), tok=True)
                    P.e("scalar", "activation", kdh[:], c.pbT[0:64, 0:128],
                        AFT.Copy, reads=("pbT",), writes=(dn,), tok=True)
                    vch = v_tm[0:64, ci, DV * h:DV * (h + 1)]
                    mm_group(c, 5, [
                        (amh[:, :], vch, pb[5][0:64, 0:DV], (an, "v_tm")),
                        (qeh[:, t0:t0 + CH], sstb[h][:], pb[5][0:64, 0:DV],
                         (qn, f"sstb{h}")),
                    ])
                    mm_group(c, 4, [(kdh[:], vch, pb[4][0:128, 0:DV],
                                     (dn, "v_tm"))])
                    P.e("scalar", "activation",
                        o_tm[0:64, ci, DV * h:DV * (h + 1)], pb[5][0:64, 0:DV],
                        AFT.Copy, reads=("pb5",), writes=("o_tm",), tok=True)
                    P.e("vector", "tensor_tensor", sst_t[:], sst[h][:],
                        pb[4][0:128, 0:DV], ALU.add,
                        reads=(f"sst{h}", "pb4"), writes=("sst_t",), tok=True)
                    P.e("vector", "tensor_scalar", sst[h][:], sst_t[:],
                        eblh[:, ci:ci + 1], None, ALU.mult,
                        reads=("sst_t", en), writes=(f"sst{h}",), tok=True)
                    P.e("scalar", "activation", sstb[h][:], sst[h][:], AFT.Copy,
                        reads=(f"sst{h}",), writes=(f"sstb{h}",), tok=True)

            NQ = 4 if NCH >= 4 else 1
            HB = NCH // NQ
            for half in range(NQ):
                ovh = o_tm[0:64, HB * half:HB * half + HB, :]
                P.e("scalar", "activation",
                    osq[0:64, 0:HB, :].rearrange("p c d -> p (c d)"),
                    ovh.rearrange("p c d -> p (c d)"), AFT.Square,
                    reads=("o_tm",), writes=("osq",), tok=True)
                P.e("vector", "tensor_reduce",
                    ssq[:, HB * half:HB * half + HB, :]
                    .rearrange("p c h -> p (c h)").unsqueeze(2),
                    osq[0:64, 0:HB, :].rearrange("p c (h d) -> p (c h) d", d=DV),
                    AXL.X, ALU.add, reads=("osq",), writes=("ssq",),
                    tok=True)
            ssqf = ssq[:].rearrange("p c h -> p (c h)")
            P.e("vector", "tensor_scalar", ssqf, ssqf, 1.0 / DV, 1e-6,
                ALU.mult, ALU.add, reads=("ssq",), writes=("ssq",), tok=True)
            P.e("scalar", "activation", ssqf, ssqf, AFT.Ln,
                reads=("ssq",), writes=("ssq",), tok=True)
            P.e("scalar", "activation", ssqf, ssqf, AFT.Exp, scale=-0.5,
                reads=("ssq",), writes=("ssq",), tok=True)
            gf = g_tm[0:64, :, :].rearrange("p c d -> p (c d)")
            P.e("scalar", "activation", gf, gf, AFT.Silu,
                reads=("g_tm",), writes=("g_tm",), tok=True)
            P.e("vector", "tensor_tensor",
                o_tm[0:64, :, :].rearrange("p c (h d) -> p c h d", d=DV),
                o_tm[0:64, :, :].rearrange("p c (h d) -> p c h d", d=DV),
                ssq[:].unsqueeze(3).broadcast_to([64, NCH, 4, DV]), ALU.mult,
                reads=("o_tm", "ssq"), writes=("o_tm",))
            ov = o_tm[0:64, :, :].rearrange("p c d -> p (c d)")
            P.e("vector", "tensor_tensor", ov, ov, gf, ALU.mult,
                reads=("o_tm", "g_tm"), writes=("o_tm",), tok=True)
            for j in range(8):
                for ci in range(NCH):
                    P.e("tensor", "transpose",
                        c.pbT[0:128, CH * ci:CH * ci + CH],
                        o_tm[0:64, ci, 128 * j:128 * (j + 1)],
                        c.identB[0:64, 0:64], reads=("o_tm", "identB"),
                        writes=("pbT",), tok=(ci == NCH - 1))
                P.e("scalar", "activation", ogT[:, j, :], c.pbT[0:128, 0:TS],
                    AFT.Copy, reads=("pbT",), writes=("ogT",), tok=True)

            for half in range(2):
                P.dma("gpsimd", wvg[:], c.Wo_t[half], writes=("wvg",))
                for k in range(8):
                    P.e("vector", "tensor_scalar", wvg[:, k, :], wvg[:, k, :],
                        c.gwS[:, k % 2:k % 2 + 1], None, ALU.mult,
                        reads=("wvg", "gwS"), writes=("wvg",), tok=(k == 7))
                for m in range(4 * half, 4 * half + 4):
                    bi = m % 2
                    mm_group(c, bi, [(wvg[:, k, 128 * m - 512 * half:
                                          128 * (m + 1) - 512 * half],
                                      ogT[:, k, :],
                                      pb[bi][:, 0:TS], ("wvg", "ogT"))
                                     for k in range(8)])
                    P.e("vector", "tensor_tensor", x2t[:], pb[bi][:, 0:TS],
                        xln[:, m, :], ALU.add,
                        reads=(f"pb{bi}", "xln"), writes=("x2t",), tok=True)
                    P.dma("sync", c.x2_hbm[m, :, ts:ts + TS], x2t[:],
                          reads=("x2t",), ch="dW")


def phase_mlp(c, wq_units=()):
    P, nc, pb = c.P, c.nc, c.pball
    T, TS, NS = c.T, c.TS, c.NS
    wqit = iter(wq_units)
    wq_next = lambda: next(wqit, lambda: None)()
    with ExitStack() as ph:
        sb = lambda n, shp, dt: ph.enter_context(nc.sbuf_tensor(n, list(shp), dt))
        x2s = sb("ml_x2s", [128, DT, TS], F32)
        stm = sb("ml_stm", [128, 4 * (TS // 128)], F32)
        c2b = sb("ml_c2b", [128, TS], F32)
        nmb = sb("ml_nmb", [128, DT, TS], BF16)
        gs16 = sb("ml_gs16", [128, TS], F16)
        m_in = sb("ml_min", [128, MIT, TS], BF16)
        mpan = [sb(f"ml_mpan{i}", [128, 8, 128], BF16) for i in range(2)]
        upan = [sb(f"ml_upan{i}", [128, 8, 128], BF16) for i in range(2)]
        dpan = [sb(f"ml_dpan{i}", [128, MIT, 128], BF16) for i in range(2)]

        for s in range(NS):
            ts = s * TS
            for i in range(DT):
                P.dma("sync", x2s[:, i, :], c.x2_hbm[i, :, ts:ts + TS],
                      writes=("ml_x2s",))
            NJ = TS // 128
            ln_stats(c, x2s[:], "ml_x2s", want_mean=False)
            ln_scalars(c, stm[:], "ml_stm", eps=1e-6, want_mean=False)
            rep_tmcol(c, c2b[:], "ml_c2b", stm[:, 0:NJ], "ml_stm", NJ)
            for i in range(DT):
                P.e("vector", "scalar_tensor_tensor", nmb[:, i, :], x2s[:, i, :],
                    c.mnwS[:, i:i + 1], c2b[:], ALU.mult, ALU.mult,
                    reads=("ml_x2s", "mnwS", "ml_c2b"), writes=("ml_nmb",),
                    tok=(i == DT - 1))
            if s == 0:
                load_panel(c, mpan[0], "ml_mpan0", c.Wmg_t[0], par=0)
                load_panel(c, upan[0], "ml_upan0", c.Wmu_t[0], par=1)
            for m in range(MIT):
                gbuf, gbn = mpan[m % 2], f"ml_mpan{m % 2}"
                ubuf, ubn = upan[m % 2], f"ml_upan{m % 2}"
                if m + 1 < MIT:
                    load_panel(c, mpan[(m + 1) % 2], f"ml_mpan{(m + 1) % 2}",
                               c.Wmg_t[m + 1], par=0)
                    load_panel(c, upan[(m + 1) % 2], f"ml_upan{(m + 1) % 2}",
                               c.Wmu_t[m + 1], par=1)
                elif s + 1 < NS:
                    load_panel(c, mpan[0], "ml_mpan0", c.Wmg_t[0], par=0)
                    load_panel(c, upan[0], "ml_upan0", c.Wmu_t[0], par=1)
                bi = m % 3
                mm_group(c, bi, [(gbuf[:, k, :], nmb[:, k, :], pb[bi][:, 0:TS],
                                  (gbn, "ml_nmb")) for k in range(8)])
                P.e("scalar", "activation", gs16[:], pb[bi][:, 0:TS], AFT.Silu,
                    reads=(f"pb{bi}",), writes=("ml_gs16",), tok=True)
                bi2 = 3 + m % 3
                mm_group(c, bi2, [(ubuf[:, k, :], nmb[:, k, :], pb[bi2][:, 0:TS],
                                   (ubn, "ml_nmb")) for k in range(8)])
                P.e("vector", "tensor_tensor", m_in[:, m, :], gs16[:],
                    pb[bi2][:, 0:TS], ALU.mult,
                    reads=("ml_gs16", f"pb{bi2}"), writes=("ml_min",), tok=True)
                wq_next()
            if s == 0:
                P.dma("gpsimd", dpan[0][:], c.Wmd_t[0], writes=("ml_dpan0",),
                      ch="dG")
            for m in range(DT):
                buf, bn = dpan[m % 2], f"ml_dpan{m % 2}"
                if m + 1 < DT:
                    P.dma("gpsimd", dpan[(m + 1) % 2][:], c.Wmd_t[m + 1],
                          writes=(f"ml_dpan{(m + 1) % 2}",),
                          ch=("dG", "dGb")[(m + 1) % 2])
                elif s + 1 < NS:
                    P.dma("gpsimd", dpan[0][:], c.Wmd_t[0], writes=("ml_dpan0",),
                          ch="dG")
                bi = 4 + m % 2
                mm_group(c, bi, [(buf[:, k, :], m_in[:, k, :], pb[bi][:, 0:TS],
                                  (bn, "ml_min")) for k in range(MIT)])
                P.e("vector", "tensor_tensor", x2s[:, m, :], pb[bi][:, 0:TS],
                    x2s[:, m, :], ALU.add,
                    reads=(f"pb{bi}", "ml_x2s"), writes=("ml_x2s",), tok=True)
                P.e("vector", "tensor_tensor", c.x0[:, m, ts:ts + TS],
                    c.x0[:, m, ts:ts + TS], x2s[:, m, :], ALU.add,
                    reads=("x0", "ml_x2s"), writes=("x0",), tok=True)
                wq_next()
        for u in wqit:
            u()


def build_wq_units(c, ph):
    P, nc, pb = c.P, c.nc, c.pball
    units = []
    if True:
        sb = lambda n, shp, dt: ph.enter_context(nc.sbuf_tensor(n, list(shp), dt))
        w1 = [sb(f"qw1{i}", [128, 16, 128], F32) for i in range(2)]
        w2 = [sb(f"qw2{i}", [128, 16, 128], F32) for i in range(2)]
        wq16 = [sb(f"wq16{i}", [128, 16, 128], F16) for i in range(2)]
        wqi16 = [sb(f"wqi16{i}", [128, 16, 128], I16) for i in range(2)]
        qdmy = [sb(f"qdmy{i}", [128, 16, 128], F16) for i in range(2)]
        acc = [sb(f"qacc{i}", [128, FIT // 2], F32) for i in range(2)]
        accr = sb("qaccr", [128, 1], F32)
        qrow = sb("qrow", [1, 4], F32)
        jobs = [("fg", c.fg_t, c.wqg_hbm, True, 0, 3),
                ("fu", c.fu_t, c.wqu_hbm, True, 1, 4),
                ("fd", c.fd_t, c.wqd_hbm, False, 2, 5)]

        def dload(buf, bn, src_t, ispan, j, ch):
            if ispan:
                P.dma("sync", buf[:].rearrange("p (a k) m -> p a k m", a=2),
                      src_t[2 * j:2 * j + 2].rearrange("a p k m -> p a k m"),
                      writes=(bn,), ch=ch)
            else:
                P.dma("sync", buf[:, 0:8, :].rearrange("p k m -> p (k m)"),
                      src_t[2 * j], writes=(bn,), ch=ch)
                P.dma("sync", buf[:, 8:16, :].rearrange("p k m -> p (k m)"),
                      src_t[2 * j + 1], writes=(bn,), ch=ch)

        def p1_panel(job, jp, j, first):
            def emit():
                (nm, src_t, dst, ispan, bcol, mcol) = job
                an = f"qacc{jp}"
                if first:
                    P.e("vector", "memset", acc[jp][:], 0.0, writes=(an,),
                        tok=True)
                buf, bn = w1[j % 2], f"qw1{j % 2}"
                dload(buf, bn, src_t, ispan, j, "dW" if j % 2 else "dWb")
                P.e("scalar", "activation",
                    qdmy[j % 2][:].rearrange("p k m -> p (k m)"),
                    buf[:].rearrange("p k m -> p (k m)"), AFT.Abs,
                    accum_out=acc[jp][:, j:j + 1],
                    reads=(bn,), writes=(an, f"qdmy{j % 2}"), tok=True)
            return emit

        def p1_tail(job, jp):
            def emit():
                (nm, src_t, dst, ispan, bcol, mcol) = job
                an = f"qacc{jp}"
                P.e("vector", "tensor_reduce", accr[:], acc[jp][:], AXL.X,
                    ALU.add, reads=(an,), writes=("qaccr",), tok=True)
                P.e("tensor", "matmul", pb[6][0:1, 0:1], c.onecolF[:], accr[:],
                    start=True, stop=True,
                    reads=("qaccr", "onecolF"), writes=("pb6",), tok=True)
                nelem = float(FIT * 8 * 128 * 128)
                P.e("vector", "tensor_scalar", qrow[0:1, 0:1], pb[6][0:1, 0:1],
                    1.0 / nelem, None, ALU.mult, reads=("pb6",),
                    writes=("qrow",), tok=True)
                P.e("vector", "tensor_scalar", qrow[0:1, 0:1], qrow[0:1, 0:1],
                    1e-5, None, ALU.max, reads=("qrow",), writes=("qrow",))
                P.e("vector", "reciprocal", qrow[0:1, 1:2], qrow[0:1, 0:1],
                    reads=("qrow",), writes=("qrow",), tok=True)
                replicate_row(c, c.betaS[:, job[4]:job[4] + 1], "betaS",
                              qrow[0:1, 1:2], "qrow")
                replicate_row(c, c.betaS[:, job[5]:job[5] + 1], "betaS",
                              qrow[0:1, 0:1], "qrow")
            return emit

        def p2_panel(job, j):
            def emit():
                (nm, src_t, dst, ispan, bcol, mcol) = job
                buf, bn = w2[j % 2], f"qw2{j % 2}"
                dload(buf, bn, src_t, ispan, j, "dR" if j % 2 else "dX")
                v = buf[:].rearrange("p k m -> p (k m)")
                qi = wqi16[j % 2][:].rearrange("p k m -> p (k m)")
                qin = f"wqi16{j % 2}"
                q = wq16[j % 2][:].rearrange("p k m -> p (k m)")
                qn = f"wq16{j % 2}"
                P.e("vector", "tensor_scalar", v, v, c.betaS[:, bcol:bcol + 1],
                    MAGIC, ALU.mult, ALU.add,
                    reads=(bn, "betaS"), writes=(bn,), tok=True)
                P.e("vector", "tensor_scalar", qi, v, MAGIC, None, ALU.subtract,
                    reads=(bn, "betaS"), writes=(qin,), tok=True)
                P.e("vector", "tensor_scalar", q, qi, 1.0, -1.0, ALU.min,
                    ALU.max, reads=(qin,), writes=(qn,), tok=True)
                if ispan:
                    P.dma("sync",
                          dst[2 * j:2 * j + 2].rearrange("a p k m -> p a k m"),
                          wq16[j % 2][:].rearrange("p (a k) m -> p a k m", a=2),
                          reads=(qn,), ch="dO")
                else:
                    P.dma("sync", dst[2 * j],
                          wq16[j % 2][:, 0:8, :].rearrange("p k m -> p (k m)"),
                          reads=(qn,), ch="dO")
                    P.dma("sync", dst[2 * j + 1],
                          wq16[j % 2][:, 8:16, :].rearrange("p k m -> p (k m)"),
                          reads=(qn,), ch="dO")
            return emit

        for ji, job in enumerate(jobs):
            jp = ji % 2
            for j in range(FIT // 2):
                units.append(p1_panel(job, jp, j, first=(j == 0)))
            units.append(p1_tail(job, jp))
            for j in range(FIT // 2):
                units.append(p2_panel(job, j))
    return units



def phase_ffn(c, src_hbm, dst_hbm, tag, add_hbm=None):
    P, nc, pb = c.P, c.nc, c.pball
    T, TS, NS = c.T, c.TS, c.NS
    NJ = TS // 128
    with ExitStack() as ph:
        sb = lambda n, shp, dt: ph.enter_context(
            nc.sbuf_tensor(f"{tag}{n}", list(shp), dt))
        nb = lambda n: f"{tag}{n}"
        xs = sb("xs", [128, DT, TS], F32)
        stm = sb("stm", [128, 4 * (TS // 128)], F32)
        cmc = sb("cmc", [128, 2, TS], F32)
        xl16 = sb("xl16", [128, DT, TS], F16)
        macc = sb("macc", [128, TS], F32)
        mxq = sb("mxq", [128, NJ], F32)
        rq = sb("rq", [128, NJ], F32)
        sq_ = sb("sq", [128, NJ], F32)
        rqb = sb("rqb", [128, TS], F16)
        gsb = sb("gsb", [128, TS], F32)
        xqi = sb("xqi", [128, TS], I16)
        xq16 = sb("xq16", [128, DT, TS], F16)
        wq = [sb(f"wq{i}", [128, 8, 128], F16) for i in range(4)]
        wu = [sb(f"wu{i}", [128, 8, 128], F16) for i in range(4)]
        wd = [sb(f"wd{i}", [128, 1024], F16) for i in range(4)]
        gs = sb("gs", [128, TS], F32)
        us = sb("us", [128, TS], F16)
        y16 = sb("y16", [128, FIT, TS], F16)
        ymacc = sb("ymacc", [128, TS], F16)
        ymn = sb("ymn", [128, TS], F16)
        ymx = sb("ymx", [128, NJ], F32)
        yfac = sb("yfac", [128, NJ], F32)
        ysc = sb("ysc", [128, NJ], F32)
        yrb = sb("yrb", [128, TS], F16)
        ysb = sb("ysb", [128, TS], F32)
        yqi = sb("yqi", [128, TS], I16)
        yq16 = y16
        if add_hbm is not None:
            dtmp = sb("dtmp", [128, 2, TS], BF16)

        for s in range(NS):
            ts = s * TS
            if src_hbm is None:
                xsl, xsl_nm = c.x0[:, :, ts:ts + TS], "x0"
            else:
                for i in range(DT):
                    P.dma("sync", xs[:, i, :], src_hbm[i, :, ts:ts + TS],
                          writes=(nb("xs"),))
                if add_hbm is not None:
                    for i in range(DT):
                        P.dma("sync", dtmp[:, i % 2, :],
                              add_hbm[i, :, ts:ts + TS],
                              writes=(nb("dtmp"),), ch="dWb")
                        P.e("vector", "tensor_tensor", xs[:, i, :],
                            dtmp[:, i % 2, :], xs[:, i, :], ALU.add,
                            reads=(nb("dtmp"), nb("xs")), writes=(nb("xs"),),
                            tok=True)
                xsl, xsl_nm = xs[:], nb("xs")

            ln_stats(c, xsl, xsl_nm)
            ln_scalars(c, stm[:], nb("stm"), eps=1e-5, want_mean=True)
            rep_tmcol(c, cmc[:], nb("cmc"), stm[:, 0:2 * NJ], nb("stm"), 2 * NJ)
            for i in range(DT):
                P.e("vector", "tensor_tensor", macc[:], xsl[:, i, :],
                    cmc[:, 0, :], ALU.mult,
                    reads=(xsl_nm, nb("cmc")), writes=(nb("macc"),),
                    tok=(i == DT - 1))
                P.e("vector", "tensor_tensor", xl16[:, i, :], macc[:],
                    cmc[:, 1, :], ALU.subtract,
                    reads=(nb("macc"), nb("cmc")),
                    writes=(nb("xl16"),), tok=(i == DT - 1))

            P.e("scalar", "activation", macc[:], xl16[:, 0, :], AFT.Abs,
                reads=(nb("xl16"),), writes=(nb("macc"),), tok=True)
            for i in range(1, DT):
                P.e("scalar", "activation", gs[:], xl16[:, i, :], AFT.Abs,
                    reads=(nb("xl16"),), writes=(nb("gs"),), tok=True)
                P.e("vector", "tensor_tensor", macc[:], macc[:], gs[:],
                    ALU.max, reads=(nb("macc"), nb("gs")),
                    writes=(nb("macc"),), tok=True)
            for j in range(NJ):
                P.e("tensor", "transpose", pb[6][0:128, 0:128],
                    macc[:, 128 * j:128 * (j + 1)], c.identS[:],
                    reads=(nb("macc"), "identS"), writes=("pb6",), tok=True)
                P.e("vector", "tensor_reduce", mxq[:, j:j + 1].unsqueeze(1),
                    pb[6][0:128, 0:128].unsqueeze(1), AXL.X, ALU.max,
                    reads=("pb6",), writes=(nb("mxq"),), tok=True)
            P.e("vector", "tensor_scalar", mxq[:], mxq[:], 1e-5, None, ALU.max,
                reads=(nb("mxq"),), writes=(nb("mxq"),))
            P.e("vector", "reciprocal", rq[:], mxq[:],
                reads=(nb("mxq"),), writes=(nb("rq"),))
            P.e("vector", "tensor_scalar", rq[:], rq[:], 127.0, None, ALU.mult,
                reads=(nb("rq"),), writes=(nb("rq"),), tok=True)
            P.e("vector", "tensor_scalar", sq_[:], mxq[:], 1.0 / 127.0, None,
                ALU.mult, reads=(nb("mxq"),), writes=(nb("sq"),), tok=True)
            rep_tmcol(c, rqb[:], nb("rqb"), rq[:], nb("rq"), NJ, cast=True)
            rep_tmcol(c, gsb[:], nb("gsb"), sq_[:], nb("sq"), NJ)
            P.e("vector", "tensor_scalar", gsb[:], gsb[:], c.betaS[:, 3:4], None,
                ALU.mult, reads=(nb("gsb"), "betaS"), writes=(nb("gsb"),),
                tok=True)
            for i in range(DT):
                P.e("vector", "tensor_tensor", macc[:], xl16[:, i, :], rqb[:],
                    ALU.mult, reads=(nb("xl16"), nb("rqb")), writes=(nb("macc"),))
                P.e("vector", "tensor_scalar", macc[:], macc[:], MAGIC, None,
                    ALU.add, reads=(nb("macc"),), writes=(nb("macc"),))
                P.e("vector", "tensor_scalar", xqi[:], macc[:], MAGIC, None,
                    ALU.subtract, reads=(nb("macc"),), writes=(nb("xqi"),))
                P.e("vector", "tensor_copy", xq16[:, i, :], xqi[:],
                    reads=(nb("xqi"), nb("rqb"), nb("macc")),
                    writes=(nb("xq16"),), tok=(i == DT - 1))

            for j in range(3):
                P.dma("sync", wq[j][:], c.wqg_hbm[j], writes=(nb(f"wq{j}"),),
                      ch=("dW", "dR")[j % 2])
                P.dma("sync", wu[j][:], c.wqu_hbm[j], writes=(nb(f"wu{j}"),),
                      ch=("dWb", "dX")[j % 2])
            for m in range(FIT):
                gbuf, gbn = wq[m % 4], nb(f"wq{m % 4}")
                ubuf, ubn = wu[m % 4], nb(f"wu{m % 4}")
                if m + 3 < FIT:
                    P.dma("sync", wq[(m + 3) % 4][:], c.wqg_hbm[m + 3],
                          writes=(nb(f"wq{(m + 3) % 4}"),),
                          ch=("dW", "dR")[(m + 3) % 2])
                    P.dma("sync", wu[(m + 3) % 4][:], c.wqu_hbm[m + 3],
                          writes=(nb(f"wu{(m + 3) % 4}"),),
                          ch=("dWb", "dX")[(m + 3) % 2])
                bi = m % 3
                mm_group(c, bi, [(gbuf[:, k, :], xq16[:, k, :], pb[bi][:, 0:TS],
                                  (gbn, nb("xq16"))) for k in range(8)])
                P.e("vector", "tensor_tensor", gs[:], pb[bi][:, 0:TS], gsb[:],
                    ALU.mult, reads=(f"pb{bi}", nb("gsb")), writes=(nb("gs"),),
                    tok=True)
                P.e("scalar", "activation", gs[:], gs[:], AFT.Silu,
                    reads=(nb("gs"),), writes=(nb("gs"),), tok=True)
                bi2 = 3 + m % 3
                mm_group(c, bi2, [(ubuf[:, k, :], xq16[:, k, :], pb[bi2][:, 0:TS],
                                   (ubn, nb("xq16"))) for k in range(8)])
                P.e("scalar", "activation", us[:], pb[bi2][:, 0:TS], AFT.Copy,
                    scale=c.betaS[:, 4:5],
                    reads=(f"pb{bi2}", "betaS"), writes=(nb("us"),), tok=True)
                P.e("vector", "tensor_tensor", y16[:, m, :], gs[:], us[:],
                    ALU.mult, reads=(nb("gs"), nb("us")), writes=(nb("y16"),),
                    tok=True)
                if m == 0:
                    P.e("vector", "tensor_copy", ymacc[:], y16[:, 0, :],
                        reads=(nb("y16"),), writes=(nb("ymacc"),), tok=True)
                    P.e("vector", "tensor_copy", ymn[:], y16[:, 0, :],
                        reads=(nb("y16"),), writes=(nb("ymn"),), tok=True)
                else:
                    P.e("vector", "tensor_tensor", ymacc[:], ymacc[:],
                        y16[:, m, :], ALU.max,
                        reads=(nb("ymacc"), nb("y16")), writes=(nb("ymacc"),),
                        tok=True)
                    P.e("vector", "tensor_tensor", ymn[:], ymn[:],
                        y16[:, m, :], ALU.min,
                        reads=(nb("ymn"), nb("y16")), writes=(nb("ymn"),),
                        tok=True)
            P.e("vector", "tensor_scalar", ymn[:], ymn[:], -1.0, None, ALU.mult,
                reads=(nb("ymn"),), writes=(nb("ymn"),))
            P.e("vector", "tensor_tensor", ymacc[:], ymacc[:], ymn[:], ALU.max,
                reads=(nb("ymacc"), nb("ymn")), writes=(nb("ymacc"),))
            P.e("vector", "tensor_copy", macc[:], ymacc[:],
                reads=(nb("ymacc"),), writes=(nb("macc"),), tok=True)
            for j in range(NJ):
                P.e("tensor", "transpose", pb[6][0:128, 0:128],
                    macc[:, 128 * j:128 * (j + 1)], c.identS[:],
                    reads=(nb("macc"), "identS"), writes=("pb6",), tok=True)
                P.e("vector", "tensor_reduce", ymx[:, j:j + 1].unsqueeze(1),
                    pb[6][0:128, 0:128].unsqueeze(1), AXL.X, ALU.max,
                    reads=("pb6",), writes=(nb("ymx"),), tok=True)
            P.e("vector", "tensor_tensor", ymx[:], ymx[:], sq_[:], ALU.mult,
                reads=(nb("ymx"), nb("sq")), writes=(nb("ymx"),))
            P.e("vector", "tensor_scalar", ymx[:], ymx[:], 1e-5, None, ALU.max,
                reads=(nb("ymx"),), writes=(nb("ymx"),))
            P.e("vector", "reciprocal", yfac[:], ymx[:],
                reads=(nb("ymx"),), writes=(nb("yfac"),))
            P.e("vector", "tensor_scalar", yfac[:], yfac[:], 127.0, None,
                ALU.mult, reads=(nb("yfac"),), writes=(nb("yfac"),))
            P.e("vector", "tensor_tensor", yfac[:], yfac[:], sq_[:], ALU.mult,
                reads=(nb("yfac"), nb("sq")), writes=(nb("yfac"),), tok=True)
            P.e("vector", "tensor_scalar", ysc[:], ymx[:], 1.0 / 127.0, None,
                ALU.mult, reads=(nb("ymx"),), writes=(nb("ysc"),))
            P.e("vector", "tensor_scalar", ysc[:], ysc[:], c.betaS[:, 5:6], None,
                ALU.mult, reads=(nb("ysc"), "betaS"), writes=(nb("ysc"),),
                tok=True)
            rep_tmcol(c, yrb[:], nb("yrb"), yfac[:], nb("yfac"), NJ, cast=True)
            rep_tmcol(c, ysb[:], nb("ysb"), ysc[:], nb("ysc"), NJ)

            def yquant(i):
                P.e("vector", "tensor_tensor", ymacc[:], y16[:, i, :], yrb[:],
                    ALU.mult, reads=(nb("y16"), nb("yrb")),
                    writes=(nb("ymacc"),))
                P.e("vector", "tensor_scalar", macc[:], ymacc[:], MAGIC, None,
                    ALU.add, reads=(nb("ymacc"),), writes=(nb("macc"),))
                P.e("vector", "tensor_scalar", yqi[:], macc[:], MAGIC, None,
                    ALU.subtract, reads=(nb("macc"),), writes=(nb("yqi"),))
                P.e("vector", "tensor_copy", yq16[:, i, :], yqi[:],
                    reads=(nb("yqi"), nb("yrb"), nb("ymacc"), nb("macc")),
                    writes=(nb("y16"),), tok=True)

            def dload(k):
                buf, bn = wd[k % 4], nb(f"wd{k % 4}")
                P.dma("sync", buf[:], c.wqd_hbm[k], writes=(bn,),
                      ch=("dW", "dWb", "dR", "dX")[k % 4])
            for mg in range(2):
                if mg == 0:
                    yquant(0)
                for k in range(3):
                    dload(k)
                for k in range(FIT):
                    if mg == 0 and k + 1 < FIT:
                        yquant(k + 1)
                    if k + 3 < FIT:
                        dload(k + 3)
                    buf, bn = wd[k % 4], nb(f"wd{k % 4}")
                    for mi in range(4):
                        m = 4 * mg + mi
                        P.e("tensor", "matmul", pb[mi][:, 0:TS],
                            buf[:, 128 * m:128 * (m + 1)], yq16[:, k, :],
                            start=(k == 0), stop=(k == FIT - 1),
                            reads=(bn, nb("y16")), writes=(f"pb{mi}",),
                            tok=True)
                for mi in range(4):
                    m = 4 * mg + mi
                    P.e("vector", "tensor_tensor", ymacc[:], pb[mi][:, 0:TS],
                        ysb[:], ALU.mult,
                        reads=(f"pb{mi}", nb("ysb")), writes=(nb("ymacc"),))
                    P.e("vector", "tensor_tensor", xs[:, m, :], ymacc[:],
                        xsl[:, m, :], ALU.add,
                        reads=(nb("ymacc"), xsl_nm, f"pb{mi}", nb("ysb")),
                        writes=(nb("xs"),), tok=True)
                    P.dma("sync", dst_hbm[m, :, ts:ts + TS], xs[:, m, :],
                          reads=(nb("xs"),),
                          ch="dO" if dst_hbm is c.y_out else "dW")



def phase_mamba(c):
    P, nc, pb = c.P, c.nc, c.pball
    T, TS, NS = c.T, c.TS, c.NS
    NJ = TS // 128
    with ExitStack() as ph:
        sb = lambda n, shp, dt: ph.enter_context(
            nc.sbuf_tensor(f"mb{n}", list(shp), dt))
        nb = lambda n: f"mb{n}"
        xs = sb("xs", [128, DT, TS], F32)
        stm = sb("stm", [128, 4 * (TS // 128)], F32)
        cmc = sb("cmc", [128, 2, TS], F32)
        xln = sb("xln", [128, DT, TS], BF16)
        wpan = [sb(f"wpan{i}", [128, 8, 128], BF16) for i in range(2)]
        wops = sb("wops", [128, 4, 16, 128], BF16)
        wxp = sb("wxp", [128, 16, 128], BF16)
        wdtp = sb("wdtp", [64, 16, 128], BF16)
        xcr = sb("xcr", [128, DINT, TS + 4], BF16)
        dtl = sb("dtl", [64, TS], BF16)
        Bsl = sb("Bsl", [32, TS], BF16)
        Csl = Bsl
        Brep = sb("Brep", [128, NST, TS], BF16)
        Crep = sb("Crep", [128, NST, TS], BF16)
        dtf = sb("dtf", [128, DINT, TS], BF16)
        dtxp = sb("dtxp", [128, TS], BF16)
        zsl = sb("zsl", [128, NST, TS], BF16)
        zev = sb("zev", [128, TS], BF16)
        dA = [sb(f"dA{i}", [128, NST, TS], BF16) for i in range(2)]
        hsl = sb("hsl", [128, NST, TS], BF16)
        hlast = sb("hlast", [128, DINT, NST], F32)
        sptmp = sb("sptmp", [128, TS], F32)
        convt = sptmp

        x_cv = lambda i: xcr[:, i, 4:4 + TS]

        P.dma("gpsimd", wxp[:], c.Wx_t, writes=(nb("wxp"),))
        P.dma("gpsimd", wdtp[:], c.Wdt_t, writes=(nb("wdtp"),))
        for j in range(4):
            P.dma("gpsimd", wops[:, j], c.Wout_t[j], writes=(nb("wops"),))
        P.e("vector", "memset", hlast[:], 0.0, writes=(nb("hlast"),), tok=True)
        P.e("vector", "memset", xcr[:, :, 0:4], 0.0, writes=(nb("xcr"),), tok=True)

        def pro_head(s):
            ts = s * TS
            for i in range(DT):
                P.dma("sync", xs[:, i, :], c.x2_hbm[i, :, ts:ts + TS],
                      writes=(nb("xs"),), ch="dWb")
            ln_stats(c, xs[:], nb("xs"))
            ln_scalars(c, stm[:], nb("stm"), eps=1e-5, want_mean=True)
            rep_tmcol(c, cmc[:], nb("cmc"), stm[:, 0:2 * NJ], nb("stm"), 2 * NJ)
            for i in range(DT):
                P.e("vector", "tensor_tensor", sptmp[:], xs[:, i, :],
                    cmc[:, 0, :], ALU.mult,
                    reads=(nb("xs"), nb("cmc")), writes=(nb("sptmp"),),
                    tok=(i == DT - 1))
                P.e("vector", "tensor_tensor", xln[:, i, :], sptmp[:],
                    cmc[:, 1, :], ALU.subtract,
                    reads=(nb("sptmp"), nb("cmc")),
                    writes=(nb("xln"),), tok=(i == DT - 1))

        def xz_tile(s, j, bank):
            buf, bnm = wpan[j % 2], nb(f"wpan{j % 2}")
            load_panel(c, buf, bnm, c.Win_t[j], par=j % 2)
            mm_group(c, bank, [(buf[:, k, :], xln[:, k, :], pb[bank][:, 0:TS],
                               (bnm, nb("xln"))) for k in range(8)])
            if j < 16:
                P.e("scalar", "activation", xcr[:, j, 4:4 + TS],
                    pb[bank][:, 0:TS], AFT.Copy,
                    reads=(f"pb{bank}",), writes=(nb("xcr"),), tok=True)
            else:
                P.e("scalar", "activation", zev[:], pb[bank][:, 0:TS], AFT.Copy,
                    reads=(f"pb{bank}",), writes=(nb("zev"),), tok=True)
                P.dma("sync", c.z_hbm[s % 2, j - 16], zev[:],
                      reads=(nb("zev"),), writes=("zhb",), ch="dW")

        def pro_tail(s):
            for i in range(DINT):
                P.e("vector", "tensor_scalar", convt[:], xcr[:, i, 1:1 + TS],
                    c.cwS[:, i, 0:1], c.cbS[:, i:i + 1], ALU.mult, ALU.add,
                    reads=(nb("xcr"), "cwS", "cbS"), writes=(nb("sptmp"),))
                for kk in range(1, 4):
                    P.e("vector", "scalar_tensor_tensor", convt[:],
                        xcr[:, i, 1 + kk:1 + kk + TS], c.cwS[:, i, kk:kk + 1],
                        convt[:], ALU.mult, ALU.add,
                        reads=(nb("xcr"), "cwS", nb("sptmp")),
                        writes=(nb("sptmp"),), tok=(kk == 3))
                P.e("vector", "tensor_copy", xcr[:, i, 0:4], xcr[:, i, TS:TS + 4],
                    reads=(nb("xcr"),), writes=(nb("xcr"),), tok=True)
                P.e("scalar", "activation", x_cv(i), convt[:], AFT.Silu,
                    reads=(nb("sptmp"), nb("xcr")), writes=(nb("xcr"),), tok=True)
            mm_group(c, 0, [(wxp[:, k, :], x_cv(k), pb[0][0:128, 0:TS],
                             (nb("wxp"), nb("xcr"))) for k in range(16)])
            P.e("scalar", "activation", dtl[:], pb[0][0:64, 0:TS], AFT.Copy,
                reads=("pb0",), writes=(nb("dtl"),), tok=True)
            P.e("scalar", "activation", Bsl[:], pb[0][64:96, 0:TS], AFT.Copy,
                reads=("pb0",), writes=(nb("Bsl"),), tok=True)
            replicate_row(c, Brep[:], nb("Brep"), Bsl[0:16, :], nb("Bsl"))
            P.e("scalar", "activation", Csl[:], pb[0][96:128, 0:TS], AFT.Copy,
                reads=("pb0",), writes=(nb("Bsl"),), tok=True)
            replicate_row(c, Crep[:], nb("Crep"), Csl[0:16, :], nb("Bsl"))
            for m in range(DINT):
                mm_group(c, 1, [(wdtp[:, m, :], dtl[:], pb[1][:, 0:TS],
                                  (nb("wdtp"), nb("dtl")))])
                P.e("scalar", "activation", sptmp[:], pb[1][:, 0:TS], AFT.Exp,
                    bias=c.bdtS[:, m:m + 1],
                    reads=("pb1", "bdtS"), writes=(nb("sptmp"),), tok=True)
                P.e("vector", "tensor_scalar", sptmp[:], sptmp[:], 1.0, None,
                    ALU.add, reads=(nb("sptmp"),), writes=(nb("sptmp"),),
                    tok=True)
                P.e("scalar", "activation", dtf[:, m, :], sptmp[:], AFT.Ln,
                    reads=(nb("sptmp"),), writes=(nb("dtf"),), tok=True)

        def em_z(p):
            pr = p % 2
            db, dn = dA[pr], nb(f"dA{pr}")
            for n in range(NST):
                P.e("scalar", "activation", db[:, n, :], dtf[:, p, :],
                    AFT.Exp, scale=c.A_sb[:, p, n:n + 1],
                    reads=(nb("dtf"), "A_sb"), writes=(dn,),
                    tok=(n == NST - 1))

        def em_scan(p):
            pr = p % 2
            zb, zn = zsl, nb("zsl")
            db, dn = dA[pr], nb(f"dA{pr}")
            P.e("vector", "tensor_tensor", dtxp[:], dtf[:, p, :],
                x_cv(p), ALU.mult,
                reads=(nb("dtf"), nb("xcr")), writes=(nb("dtxp"),), tok=True)
            P.e("vector", "tensor_tensor", zb[:], Brep[:],
                dtxp[:].unsqueeze(1).broadcast_to([128, NST, TS]),
                ALU.mult,
                reads=(nb("dtxp"), nb("Brep")), writes=(zn,), tok=True)
            for n in range(NST):
                P.e("vector", "tensor_tensor_scan", hsl[:, n, :],
                    db[:, n, :], zb[:, n, :], hlast[:, p, n:n + 1],
                    ALU.mult, ALU.add,
                    reads=(dn, zn, nb("hlast")),
                    writes=(nb("hsl"),), tok=(n == NST - 1))
            P.e("vector", "tensor_copy", hlast[:, p, :], hsl[:, :, TS - 1],
                reads=(nb("hsl"),), writes=(nb("hlast"),), tok=True)
            P.e("vector", "tensor_tensor",
                db[:].rearrange("p n t -> p (n t)"),
                hsl[:].rearrange("p n t -> p (n t)"),
                Crep[:].rearrange("p n t -> p (n t)"), ALU.mult,
                reads=(nb("hsl"), nb("Crep")), writes=(dn,), tok=True)
            for st in (8, 4, 2):
                P.e("vector", "tensor_tensor",
                    db[:, 0:st, :].rearrange("p n t -> p (n t)"),
                    db[:, 0:st, :].rearrange("p n t -> p (n t)"),
                    db[:, st:2 * st, :].rearrange("p n t -> p (n t)"),
                    ALU.add, reads=(dn,), writes=(dn,),
                    tok=(st == 2))
            P.e("vector", "tensor_tensor", dtf[:, p, :], db[:, 0, :],
                db[:, 1, :], ALU.add,
                reads=(dn,), writes=(nb("dtf"),), tok=True)
            P.e("vector", "scalar_tensor_tensor", dtf[:, p, :],
                x_cv(p), c.dpS[:, p:p + 1], dtf[:, p, :],
                ALU.mult, ALU.add,
                reads=(nb("xcr"), "dpS", nb("dtf")), writes=(nb("dtf"),),
                tok=True)

        def post_tail(s, last):
            ts = s * TS
            for i in range(DINT):
                P.dma("sync", zsl[:, i, :], c.z_hbm[s % 2, i],
                      reads=("zhb",), writes=(nb("zsl"),), ch="dWb")
            zf = zsl[:].rearrange("p d t -> p (d t)")
            P.e("scalar", "activation", zf, zf, AFT.Silu,
                reads=(nb("zsl"),), writes=(nb("zsl"),), tok=True)
            for i in range(DINT):
                P.e("vector", "tensor_tensor", dtf[:, i, :], dtf[:, i, :],
                    zsl[:, i, :], ALU.mult,
                    reads=(nb("dtf"), nb("zsl")), writes=(nb("dtf"),),
                    tok=True)
            for m in range(DT):
                if m >= 4:
                    P.dma("gpsimd", wops[:, m - 4], c.Wout_t[m],
                          writes=(nb("wops"),))
                bank = 2 + (m % 4)
                slot = m % 4
                mm_group(c, bank, [(wops[:, slot, k, :], dtf[:, k, :],
                                    pb[bank][:, 0:TS], (nb("wops"), nb("dtf")))
                                   for k in range(16)])
                P.e("vector", "tensor_copy", hsl[:, m, :], pb[bank][:, 0:TS],
                    reads=(f"pb{bank}",), writes=(nb("hsl"),), tok=True)
                P.dma("sync", c.x3_hbm[m, :, ts:ts + TS], hsl[:, m, :],
                      reads=(nb("hsl"),), ch="dW")
                if m >= 4 and not last:
                    P.dma("gpsimd", wops[:, m - 4], c.Wout_t[m - 4],
                          writes=(nb("wops"),))

        pro_head(0)
        for j in range(FIT):
            xz_tile(0, j, j % 2)
        pro_tail(0)
        for s in range(NS):
            if s + 1 < NS:
                pro_head(s + 1)
            em_z(0)
            for p in range(DINT):
                if p + 1 < DINT:
                    em_z(p + 1)
                em_scan(p)
                if s + 1 < NS:
                    xz_tile(s + 1, p, 0)
                    xz_tile(s + 1, 16 + p, 1)
            post_tail(s, last=(s + 1 == NS))
            if s + 1 < NS:
                pro_tail(s + 1)



def kernel(**inputs):
    shared = host_prepare(inputs)
    x = np.ascontiguousarray(np.asarray(inputs["L0_attn_a_hidden_states"],
                                        dtype=np.float32))
    Bn, T, _ = x.shape
    nc = build_program(T=T)
    in_maps = []
    for b in range(Bn):
        m = dict(shared)
        m["x_in"] = np.ascontiguousarray(x[b].T.reshape(DT, 128, T))
        in_maps.append(m)
    res = run_bass_kernel_spmd(nc, in_maps, core_ids=list(range(Bn)))
    out = np.stack([r["y_out"].reshape(D, T).T for r in res.results])
    return np.ascontiguousarray(out.astype(np.float32))
